# version 36
# baseline (speedup 1.0000x reference)
import numpy as np
import concourse.bass as bass
import concourse.mybir as mybir
from concourse.bass_utils import run_bass_kernel_spmd
from contextlib import ExitStack

F32 = mybir.dt.float32
BF16 = mybir.dt.bfloat16
Alu = mybir.AluOpType
Act = mybir.ActivationFunctionType
AX = mybir.AxisListType.X

B, S, D, KD, FE, NE = 4, 4096, 1024, 64, 512, 8
NPOS = 16
NKT = 32
DCH = 8
FCH = 4
TOK = NPOS * 128
NEG = -1.0e30


def core_tiles(h):
    chunks = [c for c in range(16) if (c % 4 in ((0, 3) if h == 0 else (1, 2)))]
    return sorted([t for c in chunks for t in (2 * c, 2 * c + 1)], reverse=True)


TILES = [core_tiles(0), core_tiles(1)]
E_PROG = [max(TILES[0][p], TILES[1][p]) + 1 for p in range(NPOS)]
MASK_LO = [min(TILES[0][p], TILES[1][p]) for p in range(NPOS)]
MSPAN = 3


class Sem:
    def __init__(self, nc, name):
        self.h = nc.alloc_semaphore(name=name)
        self.n = 0


class Phase:
    ENGS = ("sync", "gpsimd", "tensor", "vector", "scalar")

    def __init__(self, nc):
        self.nc = nc
        self.q = {e: [] for e in self.ENGS}

    def run(self):
        with self.nc.Block() as blk:
            for eng in self.ENGS:
                ops = self.q[eng]
                if ops:
                    def body(e, ops=ops):
                        for op in ops:
                            op(e)
                    getattr(blk, eng)(body)


def build_nc():
    nc = bass.Bass("TRN2", target_bir_lowering=False)

    x_own = nc.dram_tensor("x_own", [TOK, D], F32, kind="ExternalInput").ap()
    x_keysT = nc.dram_tensor("x_keysT", [128, 8 * DCH * 512], F32, kind="ExternalInput").ap()
    x_ownT = nc.dram_tensor("x_ownT", [128, 4 * DCH * 512], F32, kind="ExternalInput").ap()
    qidx = nc.dram_tensor("qidx", [128, NPOS], F32, kind="ExternalInput").ap()
    w_q = nc.dram_tensor("w_q", [128, DCH * KD], F32, kind="ExternalInput").ap()
    w_k = nc.dram_tensor("w_k", [128, DCH * KD], F32, kind="ExternalInput").ap()
    w_v1 = nc.dram_tensor("w_v1", [128, DCH * KD], F32, kind="ExternalInput").ap()
    w_v2 = nc.dram_tensor("w_v2", [KD, D], F32, kind="ExternalInput").ap()
    Ws1 = nc.dram_tensor("Ws1", [128, DCH * FE], F32, kind="ExternalInput").ap()
    Ws2 = nc.dram_tensor("Ws2", [128, FCH * D], F32, kind="ExternalInput").ap()
    W1 = nc.dram_tensor("W1", [NE, 128, DCH * FE], F32, kind="ExternalInput").ap()
    W2 = nc.dram_tensor("W2", [NE, 128, FCH * D], F32, kind="ExternalInput").ap()
    Wr = nc.dram_tensor("Wr", [128, DCH * NE], F32, kind="ExternalInput").ap()
    y = nc.dram_tensor("y", [TOK, D], F32, kind="ExternalOutput").ap()
    prob_dram = nc.dram_tensor("prob_dram", [NE, TOK], BF16, kind="Internal").ap()
    qT_dram = nc.dram_tensor("qT_dram", [16, 128], F32, kind="Internal").ap()

    ctx = ExitStack()
    e = ctx.enter_context

    stream = e(nc.sbuf_tensor("stream", [128, NPOS * D], F32))
    x1fm = e(nc.sbuf_tensor("x1fm", [128, DCH * TOK], BF16))
    qidx_sb = e(nc.sbuf_tensor("qidx_sb", [128, NPOS], F32))
    partidx = e(nc.sbuf_tensor("partidx", [128, 1], F32))
    identb = e(nc.sbuf_tensor("identb", [128, 128], BF16))
    identf = e(nc.sbuf_tensor("identf", [128, 128], F32))
    onesb = e(nc.sbuf_tensor("onesb", [1, 128], BF16))
    epsb = e(nc.sbuf_tensor("epsb", [128, 1], F32))
    neg8 = e(nc.sbuf_tensor("neg8", [128, 1], F32))
    bn6 = e(nc.sbuf_tensor("bn6", [128, 24], F32))
    mvall = e(nc.sbuf_tensor("mvall", [128, 2 * NPOS], F32))
    lnv = e(nc.sbuf_tensor("lnv", [128, NPOS], F32))
    rstd = e(nc.sbuf_tensor("rstd", [128, NPOS], F32))
    w1sh = e(nc.sbuf_tensor("w1sh", [128, DCH * FE], BF16))

    psA = e(nc.psum_tensor("psA", [128, 512], F32))
    psB = e(nc.psum_tensor("psB", [128, 512], F32))
    psC = e(nc.psum_tensor("psC", [128, 512], F32))
    psD = e(nc.psum_tensor("psD", [128, 512], F32))
    psT1 = e(nc.psum_tensor("psT1", [128, 1024], BF16))
    psT2 = e(nc.psum_tensor("psT2", [128, 1024], BF16))
    psO = e(nc.psum_tensor("psO", [128, 512], F32))
    psE = e(nc.psum_tensor("psE", [128, 512], F32))
    psT = [psT1, psT2]
    psH = [psA, psB, psE]

    dS = Sem(nc, "dS")
    dWx = Sem(nc, "dWx")
    dK = [Sem(nc, f"dK{i}") for i in range(3)]
    dW1 = [Sem(nc, f"dW1_{i}") for i in range(2)]
    dW2 = [Sem(nc, f"dW2_{i}") for i in range(2)]
    dPR = Sem(nc, "dPR")
    dWS = Sem(nc, "dWS")
    cT = Sem(nc, "cT")
    cV = Sem(nc, "cV")
    cS = Sem(nc, "cS")
    cG = Sem(nc, "cG")

    actx = ExitStack()
    a = actx.enter_context
    colbuf = a(nc.sbuf_tensor("colbuf", [128, MSPAN * 128], F32))
    kfm = a(nc.sbuf_tensor("kfm", [64, S], BF16))
    xv1tm = a(nc.sbuf_tensor("xv1tm", [128, NKT * (KD + 1)], BF16))
    qpos = a(nc.sbuf_tensor("qpos", [64, NPOS * 128], BF16))
    PBW = [max(E_PROG[p] for p in range(NPOS) if p % 4 == r) for r in range(4)]
    PB = [a(nc.sbuf_tensor(f"pbuf{r}", [128, PBW[r] * 128], BF16)) for r in range(4)]
    qrowb = a(nc.sbuf_tensor("qrowb", [128, NPOS * 128], F32))
    qT_sb = a(nc.sbuf_tensor("qT_sb", [16, 128], F32))
    kcols = a(nc.sbuf_tensor("kcols", [128, NPOS * MSPAN], F32))
    lrow = [a(nc.sbuf_tensor(f"lrow{i}", [1, 128], F32)) for i in range(2)]
    ones1f = a(nc.sbuf_tensor("ones1f", [1, 1], F32))
    masks_all = a(nc.sbuf_tensor("masks_all", [128, 8 * MSPAN * 128], BF16))
    lnl = a(nc.sbuf_tensor("lnl", [128, NPOS], F32))
    rl_col = a(nc.sbuf_tensor("rl_col", [128, NPOS], F32))
    ob = a(nc.sbuf_tensor("ob", [64, NPOS * 128], BF16))
    wqb = a(nc.sbuf_tensor("wqb", [128, DCH * KD], BF16))
    wkb = a(nc.sbuf_tensor("wkb", [128, DCH * KD], BF16))
    wv1b = a(nc.sbuf_tensor("wv1b", [128, DCH * KD], BF16))
    wv2b = a(nc.sbuf_tensor("wv2b", [64, D], BF16))
    x1btm = [a(nc.sbuf_tensor(f"x1btm{i}", [128, D], BF16)) for i in range(2)]

    ictx = ExitStack()
    i_ = ictx.enter_context
    x0fm = [i_(nc.sbuf_tensor(f"x0fm{i}", [128, DCH * 512], BF16)) for i in range(3)]
    xv1w = [i_(nc.sbuf_tensor(f"xv1w{i}", [64, 512], BF16)) for i in range(2)]

    P = Phase(nc)
    gq, tq, vq, sq, syq = [], [], [], [], []

    def pA_sync(sync):
        for g in range(4):
            sync.dma_start(
                out=bass.AP(stream, g * 4 * D, [[NPOS * D, 128], [D, 4], [1, D]]),
                in_=bass.AP(x_own.tensor, g * 4 * 128 * D, [[D, 128], [128 * D, 4], [1, D]]),
            ).then_inc(dS.h, 16)
        sync.dma_start(out=qidx_sb[:, :], in_=qidx[:, :]).then_inc(dS.h, 16)
    syq.append(pA_sync)
    dS.n += 16 * 5
    MK_IN = dS.n

    def pA_gp0(g):
        g.iota(colbuf[:, :], [[1, MSPAN * 128]], channel_multiplier=0,
               allow_small_or_imprecise_dtypes=True)
        g.iota(partidx[:, :], [[0, 1]], channel_multiplier=1,
               allow_small_or_imprecise_dtypes=True)
        g.memset(onesb[:, :], 1.0)
        g.memset(epsb[:, :], 1e-5)
        g.memset(neg8[:, :], -8.0)
        g.memset(xv1tm[:, :], 1.0)
        g.memset(ones1f[:, :], 1.0)
        for p_ in range(NPOS):
            g.iota(kcols[:, p_ * MSPAN:(p_ + 1) * MSPAN], [[128, MSPAN]],
                   base=MASK_LO[p_] * 128, channel_multiplier=1,
                   allow_small_or_imprecise_dtypes=True)
        for (dst, src) in ((wqb, w_q), (wkb, w_k), (wv1b, w_v1)):
            g.dma_start(out=dst[:, :], in_=src[:, :]).then_inc(dWx.h, 16)
        g.dma_start(out=wv2b[:, :], in_=w_v2[:, :]).then_inc(dWx.h, 16)
        g.drain()
        g.sem_inc(cG.h, 1)
    gq.append(pA_gp0)

    def emit_w1sh(g):
        g.dma_start(out=w1sh[:, :], in_=Ws1[:, :]).then_inc(dWS.h, 16)
    dWx.n += 16 * 4
    MK_WATT = dWx.n
    cG.n += 1
    dWS.n += 16
    MK_WS1 = dWS.n

    def pA_scalar0(s):
        s.add_instruction(mybir.InstLoadActFuncSet(
            name=nc.get_next_instruction_name(), act_func_set_id=6, ins=[], outs=[]))
    sq.append(pA_scalar0)

    def pA_consts(v):
        v.wait_ge(cG.h, 1)
        v.tensor_scalar(identb[:, :], colbuf[:, 0:128], partidx[:, :], None, Alu.is_equal)
        v.tensor_scalar(identf[:, :], colbuf[:, 0:128], partidx[:, :], None, Alu.is_equal)
        v.drain()
        v.sem_inc(cV.h, 1)
    vq.append(pA_consts)
    cV.n += 1
    MK_IDENT = cV.n
    tq.append(lambda t, m=MK_IDENT: t.wait_ge(cV.h, m))
    tq.append(lambda t, m=MK_WATT: t.wait_ge(dWx.h, m))

    psS_free = {0: (cV.h, 0), 1: (cV.h, 0), 2: (cV.h, 0)}
    psTf = {0: 0, 1: 0}
    kfm_mark = [0] * 8
    qpos_mark = [0] * 4
    xv1tm_mark = [0] * 8
    xv1w_guard = [0, 0]
    wv_proj_done = [0] * 12
    wv_dma = [0] * 12
    wno_ctr = [0]
    pbank = [0]

    def emit_wave(kind, wi):
        wno = wno_ctr[0]
        wno_ctr[0] += 1
        par = wno % 3
        if wno >= 3:
            gq.append(lambda g, m=wv_proj_done[wno - 3]: g.wait_ge(cT.h, m))

        def wdma(g, kind=kind, wi=wi, par=par):
            if kind == "own":
                src_t, pstep = x_ownT.tensor, 4 * DCH * 512
            else:
                src_t, pstep = x_keysT.tensor, 8 * DCH * 512
            g.dma_start(
                out=bass.AP(x0fm[par], 0, [[DCH * 512, 128], [1, DCH * 512]]),
                in_=bass.AP(src_t, wi * DCH * 512, [[pstep, 128], [1, DCH * 512]]),
            ).then_inc(dK[par].h, 16)
        gq.append(wdma)
        dK[par].n += 16
        wv_dma[wno] = dK[par].n
        tq.append(lambda t, m=wv_dma[wno], sh=dK[par].h: t.wait_ge(sh, m))
        projs = ((wqb, "q"),) if kind == "own" else ((wkb, "k"), (wv1b, "v"))
        for (wb, which) in projs:
            bank = pbank[0]
            pbank[0] ^= 1
            tq.append(lambda t, hv=psS_free[bank]: t.wait_ge(hv[0], hv[1]))
            for ch in range(DCH):
                def mm(t, wb=wb, ch=ch, par=par, bank=bank, st=(ch == 0), sp=(ch == DCH - 1)):
                    ins = t.matmul(psH[bank][0:64, 0:512],
                                   wb[:, ch * KD:(ch + 1) * KD],
                                   x0fm[par][:, ch * 512:(ch + 1) * 512],
                                   start=st, stop=sp)
                    if sp:
                        ins.then_inc(cT.h, 1)
                tq.append(mm)
            cT.n += 1
            mmark = cT.n
            if which == "q":
                sq.append(lambda s, m=mmark: s.wait_ge(cT.h, m))

                def qev(s, wi=wi, bank=bank):
                    s.activation(qpos[:, wi * 512:(wi + 1) * 512],
                                 psH[bank][0:64, 0:512], Act.Copy,
                                 bias=0.0, scale=0.125).then_inc(cS.h, 1)
                sq.append(qev)
                cS.n += 1
                qpos_mark[wi] = cS.n
                psS_free[bank] = (cS.h, cS.n)
            elif which == "k":
                vq.append(lambda v, m=mmark: v.wait_ge(cT.h, m))

                def kev(v, wi=wi, bank=bank):
                    v.tensor_copy(kfm[:, wi * 512:(wi + 1) * 512],
                                  psH[bank][0:64, 0:512]).then_inc(cV.h, 1)
                vq.append(kev)
                cV.n += 1
                kfm_mark[wi] = cV.n
                psS_free[bank] = (cV.h, cV.n)
            else:
                vq.append(lambda v, m=mmark: v.wait_ge(cT.h, m))
                slot = wi % 2

                def vev(v, slot=slot, bank=bank):
                    v.tensor_copy(xv1w[slot][:, :],
                                  psH[bank][0:64, 0:512]).then_inc(cV.h, 1)
                vq.append(vev)
                cV.n += 1
                vev_mark = cV.n
                psS_free[bank] = (cV.h, cV.n)

                def emit_trv(wi=wi, slot=slot, vm=vev_mark):
                    tb = wi % 2
                    tq.append(lambda t, m=vm: t.wait_ge(cV.h, m))
                    tq.append(lambda t, m=psTf[tb]: t.wait_ge(cV.h, m))
                    for kb in range(4):
                        def trv(t, kb=kb, slot=slot, tb=tb, last=(kb == 3)):
                            ins = t.transpose(psT[tb][0:128, kb * KD:(kb + 1) * KD],
                                              xv1w[slot][:, kb * 128:(kb + 1) * 128],
                                              identb[0:64, 0:64])
                            if last:
                                ins.then_inc(cT.h, 1)
                        tq.append(trv)
                    cT.n += 1
                    xv1w_guard[slot] = cT.n
                    vq.append(lambda v, m=cT.n: v.wait_ge(cT.h, m))

                    def xev(v, wi=wi, tb=tb):
                        v.tensor_copy(
                            bass.AP(xv1tm, wi * 4 * (KD + 1),
                                    [[NKT * (KD + 1), 128], [KD + 1, 4], [1, KD]]),
                            psT[tb][:, 0:256]).then_inc(cV.h, 1)
                    vq.append(xev)
                    cV.n += 1
                    xv1tm_mark[wi] = cV.n
                    psTf[tb] = cV.n
                defer_q.append([2, emit_trv])
        wv_proj_done[wno] = cT.n

    psO_free = [0, 0]
    psCD_free = [0]
    sbank_rr = [0]
    smm_mark = [[0] * 8 for _ in range(NPOS)]
    maxm = [[0] * 8 for _ in range(NPOS)]
    resc_mark = [0] * NPOS
    expm = [0] * NPOS
    ltr_mark = [0] * NPOS
    lcol_free = [0, 0]
    pvm = [0] * NPOS
    oev_mark = [0] * NPOS
    attnm = [0] * NPOS
    aev_mark = [0] * NPOS

    def chunks_of(E):
        out = []
        b0 = 0
        while b0 < E:
            blks = min(4, E - b0)
            out.append((b0, blks))
            b0 += blks
        return out

    exp_cm = [[0] * 8 for _ in range(NPOS)]

    rl_mark = [0] * NPOS

    def emit_head(p):
        E = E_PROG[p]
        mlo = MASK_LO[p]
        cl = chunks_of(E)
        mb0 = (p % 8) * MSPAN * 128
        tq.append(lambda t, m=qpos_mark[p // 4]: t.wait_ge(cS.h, m))
        for ci, (cb, blks) in enumerate(cl):
            bank = sbank_rr[0] % 3
            sbank_rr[0] += 1
            tq.append(lambda t, hv=psS_free[bank]: t.wait_ge(hv[0], hv[1]))
            kw = (cb + blks - 1) // 4
            tq.append(lambda t, m=kfm_mark[kw]: t.wait_ge(cV.h, m))

            def smm(t, p=p, cb=cb, blks=blks, bank=bank):
                for j in range(blks):
                    ins = t.matmul(psH[bank][:, j * 128:(j + 1) * 128],
                                   kfm[:, (cb + j) * 128:(cb + j + 1) * 128],
                                   qpos[:, p * 128:(p + 1) * 128],
                                   start=True, stop=True)
                    if j == blks - 1:
                        ins.then_inc(cT.h, 1)
            tq.append(smm)
            cT.n += 1
            smm_mark[p][ci] = cT.n
            mstart = max(cb, mlo)
            mend = cb + blks
            if mstart < mend:
                vq.append(lambda v, m=smm_mark[p][ci]: v.wait_ge(cT.h, m))
                vq.append(lambda v, m=mask_mark[p]: v.wait_ge(cV.h, m))

                def vmsk(v, cb=cb, bank=bank, mstart=mstart, mend=mend, mlo=mlo, mb0=mb0):
                    v.tensor_tensor(
                        psH[bank][:, (mstart - cb) * 128:(mend - cb) * 128],
                        psH[bank][:, (mstart - cb) * 128:(mend - cb) * 128],
                        masks_all[:, mb0 + (mstart - mlo) * 128: mb0 + (mend - mlo) * 128],
                        Alu.add)
                    v.drain()
                    v.sem_inc(cV.h, 1)
                vq.append(vmsk)
                cV.n += 1
                sq.append(lambda s, m=cV.n: s.wait_ge(cV.h, m))
            else:
                sq.append(lambda s, m=smm_mark[p][ci]: s.wait_ge(cT.h, m))
            if ci == 0 and p + 4 < NPOS:
                sq.append(lambda s, m=pvm[p + 4]: s.wait_ge(cT.h, m))

            def sexp(s, p=p, ci=ci, cb=cb, blks=blks, bank=bank):
                s.activation(PB[p % 4][:, cb * 128:(cb + blks) * 128],
                             psH[bank][:, 0:blks * 128], Act.Exp,
                             bias=neg8[:, :], scale=1.0).then_inc(cS.h, 1)
            sq.append(sexp)
            cS.n += 1
            exp_cm[p][ci] = cS.n
            psS_free[bank] = (cS.h, cS.n)
        expm[p] = cS.n

    def emit_tail(p):
        E = E_PROG[p]
        so = (p % 2) * 128
        lc = 256 + (p % 2)
        tq.append(lambda t, m=psO_free[p % 2]: t.wait_ge(cS.h, m))
        tq.append(lambda t, m=lcol_free[p % 2]: t.wait_ge(cS.h, m))
        if p + 1 < NPOS:
            tq.append(lambda t, m=rl_mark[p + 1]: t.wait_ge(cS.h, m))
        tq.append(lambda t, m=xv1tm_mark[(E - 1) // 4]: t.wait_ge(cV.h, m))
        nbat = (E + 7) // 8
        for bi in range(nbat):
            lo, hi = bi * 8, min(bi * 8 + 8, E)
            tq.append(lambda t, m=exp_cm[p][(hi - 1) // 4]: t.wait_ge(cS.h, m))
            for kb in range(lo, hi):
                def pv(t, kb=kb, E=E, so=so, p=p):
                    ins = t.matmul(psO[0:KD + 1, so:so + 128],
                                   xv1tm[:, kb * (KD + 1):(kb + 1) * (KD + 1)],
                                   PB[p % 4][:, kb * 128:(kb + 1) * 128],
                                   start=(kb == 0), stop=(kb == E - 1))
                    if kb == E - 1:
                        ins.then_inc(cT.h, 1)
                tq.append(pv)
        cT.n += 1
        pvm[p] = cT.n
        sq.append(lambda s, m=pvm[p]: s.wait_ge(cT.h, m))
        if p + 2 < NPOS:
            sq.append(lambda s, m=ltr_mark[p + 2]: s.wait_ge(cT.h, m))

        def oev(s, p=p, so=so):
            s.copy(ob[:, p * 128:(p + 1) * 128], psO[0:64, so:so + 128])
            s.copy(lrow[p % 2][0:1, :], psO[64:65, so:so + 128]).then_inc(cS.h, 1)
        sq.append(oev)
        cS.n += 1
        oev_mark[p] = cS.n
        psO_free[p % 2] = cS.n

    def emit_tail_fin(p):
        so = (p % 2) * 128
        lc = 256 + (p % 2)
        tq.append(lambda t, m=oev_mark[p]: t.wait_ge(cS.h, m))

        def ltr(t, p=p, lc=lc):
            t.matmul(psO[:, lc:lc + 1], lrow[p % 2][0:1, :], ones1f[0:1, 0:1],
                     start=True, stop=True).then_inc(cT.h, 1)
        tq.append(ltr)
        cT.n += 1
        ltr_mark[p] = cT.n
        sq.append(lambda s, m=ltr_mark[p]: s.wait_ge(cT.h, m))

        def srl(s, p=p, lc=lc):
            s.activation(lnl[:, p:p + 1], psO[:, lc:lc + 1], Act.Ln,
                         bias=0.0, scale=1.0)
            s.drain()
            s.activation(rl_col[:, p:p + 1], lnl[:, p:p + 1], Act.Exp,
                         bias=0.0, scale=-1.0).then_inc(cS.h, 1)
        sq.append(srl)
        cS.n += 1
        rl_mark[p] = cS.n
        lcol_free[p % 2] = cS.n
        tq.append(lambda t, m=oev_mark[p]: t.wait_ge(cS.h, m))
        tq.append(lambda t, m=psCD_free[0]: t.wait_ge(cV.h, m))

        def amm(t, p=p):
            t.matmul(psC[:, :], ob[:, p * 128:(p + 1) * 128], wv2b[:, 0:512],
                     start=True, stop=True)
            t.matmul(psD[:, :], ob[:, p * 128:(p + 1) * 128], wv2b[:, 512:1024],
                     start=True, stop=True).then_inc(cT.h, 1)
        tq.append(amm)
        cT.n += 1
        attnm[p] = cT.n
        vq.append(lambda v, m=attnm[p]: v.wait_ge(cT.h, m))
        vq.append(lambda v, m=rl_mark[p]: v.wait_ge(cS.h, m))

        def aev(v, p=p):
            v.scalar_tensor_tensor(stream[:, p * D: p * D + 512], psC[:, :],
                                   rl_col[:, p:p + 1], stream[:, p * D: p * D + 512],
                                   Alu.mult, Alu.add)
            v.scalar_tensor_tensor(stream[:, p * D + 512: p * D + 1024], psD[:, :],
                                   rl_col[:, p:p + 1], stream[:, p * D + 512: p * D + 1024],
                                   Alu.mult, Alu.add).then_inc(cV.h, 1)
        vq.append(aev)
        cV.n += 1
        aev_mark[p] = cV.n
        psCD_free[0] = cV.n

    trx_mark = [0] * NPOS
    x1fm_mark = [0] * NPOS
    stats_mark = [0] * NPOS
    cast_mark = [0] * NPOS

    def ln_stats(v, i):
        half = (i % 2) * 12
        v.bn_stats(bn6[:, half: half + 6], stream[:, i * D: i * D + 512])
        v.bn_stats(bn6[:, half + 6: half + 12], stream[:, i * D + 512: i * D + 1024])
        v.drain()
        v.bn_aggr(mvall[:, 2 * i: 2 * i + 2], bn6[:, half: half + 12]).then_inc(cV.h, 1)

    def emit_ln_stats(p):
        vq.append(lambda v, m=aev_mark[p]: v.wait_ge(cV.h, m))
        vq.append(lambda v, p=p: ln_stats(v, p))
        cV.n += 1
        stats_mark[p] = cV.n

    def emit_ln_mid(p):
        sq.append(lambda s, m=stats_mark[p]: s.wait_ge(cV.h, m))

        def srstd(s, p=p):
            s.activation(lnv[:, p:p + 1],
                         bass.AP(mvall, 2 * p + 1, [[2 * NPOS, 128], [1, 1]]),
                         Act.Ln, bias=epsb[:, :], scale=1.0)
            s.drain()
            s.activation(rstd[:, p:p + 1], lnv[:, p:p + 1], Act.Exp,
                         bias=0.0, scale=-0.5).then_inc(cS.h, 1)
        sq.append(srstd)
        cS.n += 1
        rsm = cS.n
        vq.append(lambda v, m=rsm: v.wait_ge(cS.h, m))

        def appl(v, p=p):
            v.tensor_scalar(stream[:, p * D:(p + 1) * D], stream[:, p * D:(p + 1) * D],
                            mvall[:, 2 * p: 2 * p + 1], rstd[:, p: p + 1],
                            Alu.subtract, Alu.mult).then_inc(cV.h, 1)
        vq.append(appl)
        cV.n += 1
        apm = cV.n
        sq.append(lambda s, m=apm: s.wait_ge(cV.h, m))
        if p + 2 < NPOS:
            sq.append(lambda s, m=trx_mark[p + 2]: s.wait_ge(cT.h, m))

        def cast(s, p=p):
            s.activation(x1btm[p % 2][:, :], stream[:, p * D:(p + 1) * D], Act.Copy,
                         bias=0.0, scale=1.0).then_inc(cS.h, 1)
        sq.append(cast)
        cS.n += 1
        cast_mark[p] = cS.n

    def emit_ln_back(p):
        tq.append(lambda t, m=cast_mark[p]: t.wait_ge(cS.h, m))
        tq.append(lambda t, m=psTf[p % 2]: t.wait_ge(cV.h, m))
        for ch in range(DCH):
            def trx(t, p=p, ch=ch, last=(ch == DCH - 1)):
                ins = t.transpose(psT[p % 2][:, ch * 128:(ch + 1) * 128],
                                  x1btm[p % 2][:, ch * 128:(ch + 1) * 128], identb[:, :])
                if last:
                    ins.then_inc(cT.h, 1)
            tq.append(trx)
        cT.n += 1
        trx_mark[p] = cT.n
        vq.append(lambda v, m=trx_mark[p]: v.wait_ge(cT.h, m))

        def evx(v, p=p):
            v.tensor_copy(
                bass.AP(x1fm, p * 128, [[DCH * TOK, 128], [TOK, DCH], [1, 128]]),
                psT[p % 2][:, :]).then_inc(cV.h, 1)
        vq.append(evx)
        cV.n += 1
        psTf[p % 2] = cV.n
        x1fm_mark[p] = cV.n

    vq.append(lambda v, m=MK_IN: v.wait_ge(dS.h, m))

    tq.append(lambda t: t.wait_ge(dS.h, MK_IN))
    tq.append(lambda t, m=MK_IDENT: t.wait_ge(cV.h, m))

    def qtr(t):
        t.transpose(psC[0:16, 0:128], qidx_sb[:, :], identf[:, :]).then_inc(cT.h, 1)
    tq.append(qtr)
    cT.n += 1
    qtm = cT.n
    vq.append(lambda v, m=qtm: v.wait_ge(cT.h, m))

    def qev_(v):
        v.tensor_copy(qT_sb[:, :], psC[0:16, 0:128]).then_inc(cV.h, 1)
    vq.append(qev_)
    cV.n += 1
    psCD_free[0] = cV.n
    gq.append(lambda g, m=cV.n: g.wait_ge(cV.h, m))

    def qdma(g):
        g.dma_start(out=bass.AP(qT_dram.tensor, 0, [[128, 16], [1, 128]]),
                    in_=qT_sb[:, :]).then_inc(dPR.h, 16)
    gq.append(qdma)
    dPR.n += 16
    gq.append(lambda g, m=dPR.n: g.wait_ge(dPR.h, m))

    def qbc(g):
        g.dma_start(out=qrowb[:, :],
                    in_=bass.AP(qT_dram.tensor, 0, [[0, 128], [1, NPOS * 128]])
                    ).then_inc(dPR.h, 16)
    gq.append(qbc)
    dPR.n += 16
    qrow_mark = dPR.n
    mask_mark = [0] * NPOS

    def emit_gmask(p):
        vq.append(lambda v, m=qrow_mark: v.wait_ge(dPR.h, m))

        def gmask(v, p=p):
            base = (p % 8) * MSPAN * 128
            for j in range(MSPAN):
                ins = v.tensor_scalar(
                    masks_all[:, base + j * 128: base + (j + 1) * 128],
                    qrowb[:, p * 128:(p + 1) * 128],
                    kcols[:, p * MSPAN + j: p * MSPAN + j + 1],
                    NEG, Alu.is_lt, Alu.mult)
                if j == MSPAN - 1:
                    ins.then_inc(cV.h, 1)
        vq.append(gmask)
        cV.n += 1
        mask_mark[p] = cV.n
    SCHED = [("key", 0), ("own", 3), 15, 14, ("key", 1), 13, 12,
             ("own", 2), ("key", 2), 11, 10, ("key", 3), 9, 8,
             ("own", 1), ("key", 4), 7, 6, ("key", 5), 5, 4,
             ("own", 0), ("key", 6), 3, 2, ("key", 7), 1, 0]
    head_fifo, ln_fifo, st2, st3 = [], [], [], []
    defer_q = []

    def flush_defer():
        keep = []
        for item in defer_q:
            item[0] -= 1
            if item[0] <= 0:
                item[1]()
            else:
                keep.append(item)
        defer_q[:] = keep

    def flush_defer_all():
        while defer_q:
            flush_defer()

    def pump_ln():
        lp = ln_fifo.pop(0)
        emit_ln_stats(lp)
        st2.append(lp)
        if len(st2) > 1:
            l2 = st2.pop(0)
            emit_ln_mid(l2)
            st3.append(l2)
            if len(st3) > 1:
                emit_ln_back(st3.pop(0))

    def pump_tail():
        qp = head_fifo.pop(0)
        emit_tail(qp)
        emit_tail_fin(qp)
        ln_fifo.append(qp)
        if len(ln_fifo) > 2:
            pump_ln()

    for evno, ev in enumerate(SCHED):
        flush_defer()
        if isinstance(ev, tuple):
            emit_wave(*ev)
        else:
            qp = None
            if len(head_fifo) >= 3:
                qp = head_fifo.pop(0)
                emit_tail(qp)
            emit_head(ev)
            if ev >= 8:
                defer_q.append([1, lambda p=ev - 8: emit_gmask(p)])
            head_fifo.append(ev)
            if qp is not None:
                emit_tail_fin(qp)
                ln_fifo.append(qp)
                if len(ln_fifo) > 2:
                    pump_ln()
        if evno == 1:
            gq.append(lambda g: emit_w1sh(g))
            for p_ in reversed(range(8, NPOS)):
                emit_gmask(p_)
    flush_defer_all()
    while head_fifo:
        pump_tail()
    while ln_fifo:
        pump_ln()
    while st2:
        l2 = st2.pop(0)
        emit_ln_mid(l2)
        st3.append(l2)
    while st3:
        emit_ln_back(st3.pop(0))

    P.q["sync"].append(lambda sy, q=syq: [f(sy) for f in q])
    P.q["gpsimd"].append(lambda g, q=gq: [f(g) for f in q])
    P.q["tensor"].append(lambda t, q=tq: [f(t) for f in q])
    P.q["vector"].append(lambda v, q=vq: [f(v) for f in q])
    P.q["scalar"].append(lambda s, q=sq: [f(s) for f in q])
    P.run()
    ictx.close()
    actx.close()

    fctx = ExitStack()
    f_ = fctx.enter_context
    wrb = f_(nc.sbuf_tensor("wrb", [128, DCH * NE], BF16))
    w1s = [f_(nc.sbuf_tensor(f"w1s{i}", [128, DCH * FE], BF16)) for i in range(2)]
    w2s = [f_(nc.sbuf_tensor(f"w2s{i}", [128, FCH * D], BF16)) for i in range(2)]
    hr = [f_(nc.sbuf_tensor(f"hr{i}", [128, 512], BF16)) for i in range(2)]
    hq = [f_(nc.sbuf_tensor(f"hq{i}", [128, 512], BF16)) for i in range(2)]
    bcastP = [f_(nc.sbuf_tensor(f"bcast{i}", [128, TOK], BF16)) for i in range(2)]
    probfm_sb = f_(nc.sbuf_tensor("probfm_sb", [8, TOK], BF16))

    rctx = ExitStack()
    r_ = rctx.enter_context
    lfm = r_(nc.sbuf_tensor("lfm", [8, TOK], F32))
    logits_tm = r_(nc.sbuf_tensor("logits_tm", [128, NPOS * NE], F32))
    ltop = r_(nc.sbuf_tensor("ltop", [128, NPOS * NE], F32))
    probtm = r_(nc.sbuf_tensor("probtm", [128, NPOS * NE], F32))
    probtmb = r_(nc.sbuf_tensor("probtmb", [128, NPOS * NE], BF16))
    dbuf = r_(nc.sbuf_tensor("dbuf", [128, NPOS], F32))
    ubuf = r_(nc.sbuf_tensor("ubuf", [128, NPOS], F32))
    abuf = r_(nc.sbuf_tensor("abuf", [128, NPOS], F32))
    rbuf = r_(nc.sbuf_tensor("rbuf", [128, NPOS], F32))
    gbuf = r_(nc.sbuf_tensor("gbuf", [128, NPOS], F32))

    hctx = ExitStack()
    h_ = hctx.enter_context
    H2a = h_(nc.sbuf_tensor("H2a", [128, FCH * TOK], BF16))
    H2b = h_(nc.sbuf_tensor("H2b", [128, FCH * TOK], BF16))

    P = Phase(nc)
    vq, sq, tq, gq, syq = [], [], [], [], []

    NM = NE + 1
    w1_mark = [0] * NM
    w2_mark = [0] * NM

    def load_w1(g, memb):
        g.dma_start(out=w1s[(memb - 1) % 2][:, :],
                    in_=bass.AP(W1.tensor, (memb - 1) * 128 * DCH * FE,
                                [[DCH * FE, 128], [1, DCH * FE]])
                    ).then_inc(dW1[(memb - 1) % 2].h, 16)

    def load_w2(g, memb):
        if memb == 0:
            src_t, off = Ws2.tensor, 0
        else:
            src_t, off = W2.tensor, (memb - 1) * 128 * FCH * D
        g.dma_start(out=w2s[memb % 2][:, :],
                    in_=bass.AP(src_t, off, [[FCH * D, 128], [1, FCH * D]])
                    ).then_inc(dW2[memb % 2].h, 16)

    def sched_w1(memb):
        dW1[(memb - 1) % 2].n += 16
        w1_mark[memb] = dW1[(memb - 1) % 2].n

    def sched_w2(memb):
        dW2[memb % 2].n += 16
        w2_mark[memb] = dW2[memb % 2].n

    def pB_gp(g):
        g.dma_start(out=wrb[:, :], in_=Wr[:, :]).then_inc(dWx.h, 16)
    gq.append(pB_gp)
    dWx.n += 16
    MK_WR = dWx.n
    gq.append(lambda g: load_w2(g, 0))
    sched_w2(0)
    for memb in (1, 2):
        gq.append(lambda g, memb=memb: load_w1(g, memb))
        sched_w1(memb)
    gq.append(lambda g: load_w2(g, 1))
    sched_w2(1)

    def ln_statsB(v, i):
        half = (i % 2) * 12
        v.bn_stats(bn6[:, half: half + 6], stream[:, i * D: i * D + 512])
        v.bn_stats(bn6[:, half + 6: half + 12], stream[:, i * D + 512: i * D + 1024])
        v.drain()
        v.bn_aggr(mvall[:, 2 * i: 2 * i + 2], bn6[:, half: half + 12]).then_inc(cV.h, 1)

    psHf = {0: (cV.h, 0), 1: (cV.h, 0)}
    hq_free = {0: 0, 1: 0}
    psCD2 = {0: 0, 1: 0}
    h2_guard = {id(H2a): 0, id(H2b): 0}
    bcast_free = {0: 0, 1: 0}
    mm1_done = [0] * NM
    mm2_done = [0] * NM
    prow_mark = [0] * NM
    probfm_mark = [0]
    resid2_mark = [0] * NPOS

    prob_st = [0]

    def gen_bcast(memb):
        buf = bcastP[memb % 2]
        if prob_st[0] == 0:
            gq.append(lambda g, m=probfm_mark[0]: g.wait_ge(cV.h, m))

            def pst(g):
                g.dma_start(out=bass.AP(prob_dram.tensor, 0, [[TOK, 8], [1, TOK]]),
                            in_=probfm_sb[:, :]).then_inc(dPR.h, 16)
            gq.append(pst)
            dPR.n += 16
            prob_st[0] = dPR.n
        gq.append(lambda g, m=dPR.n: g.wait_ge(dPR.h, m))
        gq.append(lambda g, m=bcast_free[memb % 2]: g.wait_ge(cV.h, m))

        def bdma(g, memb=memb, buf=buf):
            g.dma_start(out=buf[:, :],
                        in_=bass.AP(prob_dram.tensor, (memb - 1) * TOK,
                                    [[0, 128], [1, TOK]])).then_inc(dPR.h, 16)
        gq.append(bdma)
        dPR.n += 16
        prow_mark[memb] = dPR.n

    def mm1(memb, H2x):
        h2t4 = [None] * 4
        if memb == 0:
            tq.append(lambda t: t.wait_ge(dWS.h, MK_WS1))
            w1buf = w1sh
        else:
            tq.append(lambda t, m=w1_mark[memb], sh=dW1[(memb - 1) % 2].h: t.wait_ge(sh, m))
            w1buf = w1s[(memb - 1) % 2]
        tq.append(lambda t, m=h2_guard[id(H2x)]: t.wait_ge(cT.h, m))
        for t4 in range(4):
            for fc in range(FCH):
                idx = t4 * FCH + fc
                bank = idx % 2
                tq.append(lambda t, hv=psHf[bank]: t.wait_ge(hv[0], hv[1]))
                for ch in range(DCH):
                    def hmm(t, fc=fc, t4=t4, ch=ch, bank=bank, w1buf=w1buf,
                            st=(ch == 0), sp=(ch == DCH - 1)):
                        ins = t.matmul(psH[bank][:, :],
                                       w1buf[:, ch * FE + fc * 128: ch * FE + (fc + 1) * 128],
                                       x1fm[:, ch * TOK + t4 * 512: ch * TOK + (t4 + 1) * 512],
                                       start=st, stop=sp)
                        if sp:
                            ins.then_inc(cT.h, 1)
                    tq.append(hmm)
                cT.n += 1
                gmark = cT.n
                hslot = idx % 2
                sq.append(lambda s, m=gmark: s.wait_ge(cT.h, m))

                def hrel(s, bank=bank, hslot=hslot):
                    s.activation(hr[hslot][:, :], psH[bank][:, :], Act.Relu,
                                 bias=0.0, scale=1.0).then_inc(cS.h, 1)
                sq.append(hrel)
                cS.n += 1
                psHf[bank] = (cS.h, cS.n)
                if memb != 0:
                    sq.append(lambda s, m=hq_free[hslot]: s.wait_ge(cV.h, m))

                def hsqr(s, fc=fc, t4=t4, hslot=hslot, H2x=H2x, memb=memb):
                    s.drain()
                    if memb == 0:
                        dst = H2x[:, fc * TOK + t4 * 512: fc * TOK + (t4 + 1) * 512]
                    else:
                        dst = hq[hslot][:, :]
                    s.activation(dst, hr[hslot][:, :], Act.Square,
                                 bias=0.0, scale=1.0).then_inc(cS.h, 1)
                sq.append(hsqr)
                cS.n += 1
                sqm = cS.n
                if memb != 0:
                    vq.append(lambda v, m=sqm: v.wait_ge(cS.h, m))
                    if idx == 0:
                        vq.append(lambda v, m=prow_mark[memb], h=dPR.h:
                                  v.wait_ge(h, m))

                    def hbc(v, fc=fc, t4=t4, hslot=hslot, H2x=H2x, memb=memb):
                        v.tensor_tensor(
                            H2x[:, fc * TOK + t4 * 512: fc * TOK + (t4 + 1) * 512],
                            hq[hslot][:, :],
                            bcastP[memb % 2][:, t4 * 512:(t4 + 1) * 512],
                            Alu.mult).then_inc(cV.h, 1)
                    vq.append(hbc)
                    cV.n += 1
                    hq_free[hslot] = cV.n
            if memb == 0:
                h2t4[t4] = (cS.h, cS.n)
            else:
                h2t4[t4] = (cV.h, cV.n)
        if memb != 0:
            bcast_free[memb % 2] = cV.n
        mm1_done[memb] = cT.n
        return h2t4

    st_marks = [0] * NPOS

    def emit_ln2_store(tt):
        sq.append(lambda s, m=resid2_mark[tt]: s.wait_ge(cV.h, m))

        def rstd2(s, tt=tt):
            s.activation(lnv[:, tt:tt + 1],
                         bass.AP(mvall, 2 * tt + 1, [[2 * NPOS, 128], [1, 1]]),
                         Act.Ln, bias=epsb[:, :], scale=1.0)
            s.drain()
            s.activation(rstd[:, tt:tt + 1], lnv[:, tt:tt + 1], Act.Exp,
                         bias=0.0, scale=-0.5).then_inc(cS.h, 1)
        sq.append(rstd2)
        cS.n += 1
        rm = cS.n
        vq.append(lambda v, m=rm: v.wait_ge(cS.h, m))

        def appl2(v, tt=tt):
            v.tensor_scalar(stream[:, tt * D:(tt + 1) * D], stream[:, tt * D:(tt + 1) * D],
                            mvall[:, 2 * tt: 2 * tt + 1], rstd[:, tt: tt + 1],
                            Alu.subtract, Alu.mult).then_inc(cV.h, 1)
        vq.append(appl2)
        cV.n += 1
        st_marks[tt] = cV.n
        syq.append(lambda sy, m=st_marks[tt]: sy.wait_ge(cV.h, m))

        def sto(sy, tt=tt):
            sy.dma_start(out=bass.AP(y.tensor, tt * 128 * D, [[D, 128], [1, D]]),
                         in_=stream[:, tt * D:(tt + 1) * D]).then_inc(dS.h, 16)
        syq.append(sto)
        dS.n += 16

    def mm2(members, h2s, h2marks, final=False):
        for memb in members:
            tq.append(lambda t, m=w2_mark[memb], sh=dW2[memb % 2].h: t.wait_ge(sh, m))
        nm = len(members)
        for tt in range(NPOS):
            if tt % 4 == 0:
                for hm in h2marks:
                    tq.append(lambda t, hv=hm[tt // 4]: t.wait_ge(hv[0], hv[1]))
            for half in range(2):
                ps = psC if half == 0 else psD
                tq.append(lambda t, m=psCD2[half]: t.wait_ge(cV.h, m))
                for mi in range(nm):
                    H2x = h2s[mi]
                    w2buf = w2s[members[mi] % 2]
                    for c in range(FCH):
                        def omm(t, tt=tt, half=half, c=c, H2x=H2x, w2buf=w2buf, ps=ps,
                                st=(mi == 0 and c == 0), sp=(mi == nm - 1 and c == FCH - 1)):
                            ins = t.matmul(ps[:, :],
                                           H2x[:, c * TOK + tt * 128: c * TOK + (tt + 1) * 128],
                                           w2buf[:, c * D + half * 512: c * D + half * 512 + 512],
                                           start=st, stop=sp)
                            if sp:
                                ins.then_inc(cT.h, 1)
                        tq.append(omm)
                cT.n += 1
                vq.append(lambda v, m=cT.n: v.wait_ge(cT.h, m))

                def oev2(v, tt=tt, half=half, ps=ps):
                    v.tensor_tensor(stream[:, tt * D + half * 512: tt * D + half * 512 + 512],
                                    stream[:, tt * D + half * 512: tt * D + half * 512 + 512],
                                    ps[:, :], Alu.add).then_inc(cV.h, 1)
                vq.append(oev2)
                cV.n += 1
                psCD2[half] = cV.n
            if final:
                vq.append(lambda v: v.drain())
                vq.append(lambda v, tt=tt: ln_statsB(v, tt))
                cV.n += 1
                resid2_mark[tt] = cV.n
                if tt > 0:
                    emit_ln2_store(tt - 1)
        if final:
            emit_ln2_store(NPOS - 1)
        for memb in members:
            mm2_done[memb] = cT.n
        return cT.n

    h2a_marks = mm1(0, H2a)
    tq.append(lambda t: t.wait_ge(dWx.h, MK_WR))
    for j in range(4):
        bank = j % 2
        tq.append(lambda t, hv=psHf[bank]: t.wait_ge(hv[0], hv[1]))
        for ch in range(DCH):
            def lmm(t, ch=ch, j=j, bank=bank, st=(ch == 0), sp=(ch == DCH - 1)):
                ins = t.matmul(psH[bank][0:8, 0:512], wrb[:, ch * NE:(ch + 1) * NE],
                               x1fm[:, ch * TOK + j * 512: ch * TOK + j * 512 + 512],
                               start=st, stop=sp)
                if sp:
                    ins.then_inc(cT.h, 1)
            tq.append(lmm)
        cT.n += 1
        vq.append(lambda v, m=cT.n: v.wait_ge(cT.h, m))

        def lev(v, j=j, bank=bank):
            v.tensor_copy(lfm[:, j * 512:(j + 1) * 512], psH[bank][0:8, 0:512]
                          ).then_inc(cV.h, 1)
        vq.append(lev)
        cV.n += 1
        psHf[bank] = (cV.h, cV.n)
    lev_mark = cV.n
    done = mm2([0], [H2a], [h2a_marks])
    tq.append(lambda t, m=lev_mark: t.wait_ge(cV.h, m))
    for i in range(NPOS):
        def ltr(t, i=i, last=(i == NPOS - 1)):
            ins = t.transpose(psE[:, i * 8:(i + 1) * 8], lfm[:, i * 128:(i + 1) * 128],
                              identf[0:8, 0:8])
            if last:
                ins.then_inc(cT.h, 1)
        tq.append(ltr)
    cT.n += 1
    vq.append(lambda v, m=cT.n: v.wait_ge(cT.h, m))

    def p5_prob(v):
        v.tensor_copy(logits_tm[:, :], psE[:, 0:NPOS * NE])
        v.drain()
        for i in range(NPOS):
            v.max(ltop[:, i * 8:(i + 1) * 8], logits_tm[:, i * 8:(i + 1) * 8])
        v.drain()
        v.tensor_tensor(dbuf[:, :], bass.AP(ltop, 1, [[NPOS * NE, 128], [NE, NPOS]]),
                        bass.AP(ltop, 0, [[NPOS * NE, 128], [NE, NPOS]]), Alu.subtract)
        v.drain()
        v.sem_inc(cV.h, 1)
    vq.append(p5_prob)
    cV.n += 1
    sq.append(lambda s, m=cV.n: s.wait_ge(cV.h, m))

    def p5_exp(s):
        s.activation(ubuf[:, :], dbuf[:, :], Act.Exp, bias=0.0, scale=1.0)
        s.drain()
        s.sem_inc(cS.h, 1)
    sq.append(p5_exp)
    cS.n += 1
    vq.append(lambda v, m=cS.n: v.wait_ge(cS.h, m))

    def p5_gate(v):
        v.tensor_scalar(abuf[:, :], ubuf[:, :], 1.0, None, Alu.add)
        v.drain()
        v.reciprocal(rbuf[:, :], abuf[:, :])
        v.drain()
        v.tensor_scalar(gbuf[:, :], rbuf[:, :], 1.0, None, Alu.is_ge)
        v.drain()
        for i in range(NPOS):
            v.tensor_scalar(probtm[:, i * 8:(i + 1) * 8], logits_tm[:, i * 8:(i + 1) * 8],
                            ltop[:, i * 8:i * 8 + 1], gbuf[:, i:i + 1],
                            Alu.is_equal, Alu.mult)
        v.drain()
        v.tensor_copy(probtmb[:, :], probtm[:, :])
        v.drain()
        v.sem_inc(cV.h, 1)
    vq.append(p5_gate)
    cV.n += 1
    tq.append(lambda t, m=cV.n: t.wait_ge(cV.h, m))
    psTfB = {0: 0, 1: 0}
    for bat in range(2):
        tq.append(lambda t, m=psTfB[bat]: t.wait_ge(cV.h, m))
        for i8 in range(8):
            i = bat * 8 + i8
            def ptr2(t, i=i, i8=i8, bat=bat, last=(i8 == 7)):
                ins = t.transpose(psT[bat][0:8, i8 * 128:(i8 + 1) * 128],
                                  probtmb[:, i * 8:(i + 1) * 8], identb[:, :])
                if last:
                    ins.then_inc(cT.h, 1)
            tq.append(ptr2)
        cT.n += 1
        vq.append(lambda v, m=cT.n: v.wait_ge(cT.h, m))

        def pev2(v, bat=bat):
            v.tensor_copy(probfm_sb[:, bat * 1024:(bat + 1) * 1024],
                          psT[bat][0:8, 0:1024]).then_inc(cV.h, 1)
        vq.append(pev2)
        cV.n += 1
        psTfB[bat] = cV.n
    probfm_mark[0] = cV.n
    h2_guard[id(H2a)] = done
    gq.append(lambda g, m=done: g.wait_ge(cT.h, m))
    gq.append(lambda g: load_w2(g, 2))
    sched_w2(2)


    gen_bcast(1)
    gen_bcast(2)
    for g_ in range(4):
        ma, mb = 2 * g_ + 1, 2 * g_ + 2
        bufa = H2b if ma % 2 == 1 else H2a
        bufb = H2b if mb % 2 == 1 else H2a
        ma_marks = mm1(ma, bufa)
        if ma + 2 < NM:
            gq.append(lambda g, m=mm1_done[ma]: g.wait_ge(cT.h, m))
            gq.append(lambda g, memb=ma + 2: load_w1(g, memb))
            sched_w1(ma + 2)
            gen_bcast(ma + 2)
        mb_marks = mm1(mb, bufb)
        if mb + 2 < NM:
            gen_bcast(mb + 2)
        if mb + 2 < NM:
            gq.append(lambda g, m=mm1_done[mb]: g.wait_ge(cT.h, m))
            gq.append(lambda g, memb=mb + 2: load_w1(g, memb))
            sched_w1(mb + 2)
        done = mm2([ma, mb], [bufa, bufb], [ma_marks, mb_marks], final=(g_ == 3))
        h2_guard[id(H2a)] = done
        h2_guard[id(H2b)] = done
        for nxt in (ma + 2, mb + 2):
            if nxt < NM:
                gq.append(lambda g, m=done: g.wait_ge(cT.h, m))
                gq.append(lambda g, memb=nxt: load_w2(g, memb))
                sched_w2(nxt)

    syq.append(lambda sy, m=dS.n: sy.wait_ge(dS.h, m))

    P.q["gpsimd"].append(lambda g, q=gq: [f(g) for f in q])
    P.q["vector"].append(lambda v, q=vq: [f(v) for f in q])
    P.q["scalar"].append(lambda s, q=sq: [f(s) for f in q])
    P.q["tensor"].append(lambda t, q=tq: [f(t) for f in q])
    P.q["sync"].append(lambda sy, q=syq: [f(sy) for f in q])
    P.run()

    hctx.close()
    rctx.close()
    fctx.close()
    ctx.close()
    return nc


_NC_CACHE = None


def _get_nc():
    global _NC_CACHE
    if _NC_CACHE is None:
        _NC_CACHE = build_nc()
    return _NC_CACHE


def kernel(L0_attn_x, L0_attn_w_q, L0_attn_w_k, L0_attn_w_v1, L0_attn_w_v2,
           L0_ffn_Ws1, L0_ffn_Ws2, L0_ffn_W1, L0_ffn_W2, L0_ffn_Wr):
    x = np.ascontiguousarray(np.asarray(L0_attn_x, dtype=np.float32))
    def pm(a, nch):
        a = np.asarray(a, np.float32)
        return np.ascontiguousarray(
            a.reshape(nch, 128, a.shape[-1]).transpose(1, 0, 2).reshape(128, -1))

    w = {
        "w_q": pm(L0_attn_w_q, DCH),
        "w_k": pm(L0_attn_w_k, DCH),
        "w_v1": pm(L0_attn_w_v1, DCH),
        "w_v2": np.ascontiguousarray(np.asarray(L0_attn_w_v2, np.float32)),
        "Ws1": pm(L0_ffn_Ws1, DCH),
        "Ws2": pm(L0_ffn_Ws2, FCH),
        "W1": np.ascontiguousarray(np.stack(
            [pm(np.asarray(L0_ffn_W1, np.float32)[e], DCH) for e in range(NE)])),
        "W2": np.ascontiguousarray(np.stack(
            [pm(np.asarray(L0_ffn_W2, np.float32)[e], FCH) for e in range(NE)])),
        "Wr": pm(L0_ffn_Wr, DCH),
    }
    nc = _get_nc()
    xW = [np.ascontiguousarray(
        x[b].reshape(8, 512, 8, 128).transpose(3, 0, 2, 1).reshape(128, 8 * DCH * 512))
        for b in range(B)]
    in_maps = []
    for core in range(8):
        b, h = core // 2, core % 2
        tiles = TILES[h]
        xb = x[b]
        xo = np.ascontiguousarray(xb.reshape(NKT, 128, D)[tiles].reshape(TOK, D))
        qi = (np.asarray(tiles, np.float32)[None, :] * 128.0
              + np.arange(128, dtype=np.float32)[:, None])
        xoW = np.ascontiguousarray(
            xo.reshape(4, 512, 8, 128).transpose(3, 0, 2, 1).reshape(128, 4 * DCH * 512))
        m = {"x_own": xo, "x_keysT": xW[b], "x_ownT": xoW,
             "qidx": np.ascontiguousarray(qi)}
        m.update(w)
        in_maps.append(m)
    res = run_bass_kernel_spmd(nc, in_maps, core_ids=list(range(8)))
    out = np.empty((B, S, D), dtype=np.float32)
    for core in range(8):
        b, h = core // 2, core % 2
        tiles = TILES[h]
        yb = np.asarray(res.results[core]["y"]).reshape(NPOS, 128, D)
        out[b].reshape(NKT, 128, D)[tiles] = yb
    return out


# version 37 (final)
# speedup vs baseline: 1.0102x; 1.0102x over previous
import numpy as np
import concourse.bass as bass
import concourse.mybir as mybir
from concourse.bass_utils import run_bass_kernel_spmd
from contextlib import ExitStack

F32 = mybir.dt.float32
BF16 = mybir.dt.bfloat16
Alu = mybir.AluOpType
Act = mybir.ActivationFunctionType
AX = mybir.AxisListType.X

B, S, D, KD, FE, NE = 4, 4096, 1024, 64, 512, 8
NPOS = 16
NKT = 32
DCH = 8
FCH = 4
TOK = NPOS * 128
NEG = -1.0e30


def core_tiles(h):
    chunks = [c for c in range(16) if (c % 4 in ((0, 3) if h == 0 else (1, 2)))]
    return sorted([t for c in chunks for t in (2 * c, 2 * c + 1)], reverse=True)


TILES = [core_tiles(0), core_tiles(1)]
E_PROG = [max(TILES[0][p], TILES[1][p]) + 1 for p in range(NPOS)]
MASK_LO = [min(TILES[0][p], TILES[1][p]) for p in range(NPOS)]
MSPAN = 3


class Sem:
    def __init__(self, nc, name):
        self.h = nc.alloc_semaphore(name=name)
        self.n = 0


class Phase:
    ENGS = ("sync", "gpsimd", "tensor", "vector", "scalar")

    def __init__(self, nc):
        self.nc = nc
        self.q = {e: [] for e in self.ENGS}

    def run(self):
        with self.nc.Block() as blk:
            for eng in self.ENGS:
                ops = self.q[eng]
                if ops:
                    def body(e, ops=ops):
                        for op in ops:
                            op(e)
                    getattr(blk, eng)(body)


def build_nc():
    nc = bass.Bass("TRN2", target_bir_lowering=False)

    x_own = nc.dram_tensor("x_own", [TOK, D], F32, kind="ExternalInput").ap()
    x_keysT = nc.dram_tensor("x_keysT", [128, 8 * DCH * 512], F32, kind="ExternalInput").ap()
    x_ownT = nc.dram_tensor("x_ownT", [128, 4 * DCH * 512], F32, kind="ExternalInput").ap()
    qidx = nc.dram_tensor("qidx", [128, NPOS], F32, kind="ExternalInput").ap()
    w_q = nc.dram_tensor("w_q", [128, DCH * KD], F32, kind="ExternalInput").ap()
    w_k = nc.dram_tensor("w_k", [128, DCH * KD], F32, kind="ExternalInput").ap()
    w_v1 = nc.dram_tensor("w_v1", [128, DCH * KD], F32, kind="ExternalInput").ap()
    w_v2 = nc.dram_tensor("w_v2", [KD, D], F32, kind="ExternalInput").ap()
    Ws1 = nc.dram_tensor("Ws1", [128, DCH * FE], F32, kind="ExternalInput").ap()
    Ws2 = nc.dram_tensor("Ws2", [128, FCH * D], F32, kind="ExternalInput").ap()
    W1 = nc.dram_tensor("W1", [NE, 128, DCH * FE], F32, kind="ExternalInput").ap()
    W2 = nc.dram_tensor("W2", [NE, 128, FCH * D], F32, kind="ExternalInput").ap()
    Wr = nc.dram_tensor("Wr", [128, DCH * NE], F32, kind="ExternalInput").ap()
    y = nc.dram_tensor("y", [TOK, D], F32, kind="ExternalOutput").ap()
    prob_dram = nc.dram_tensor("prob_dram", [NE, TOK], BF16, kind="Internal").ap()
    qT_dram = nc.dram_tensor("qT_dram", [16, 128], F32, kind="Internal").ap()

    ctx = ExitStack()
    e = ctx.enter_context

    stream = e(nc.sbuf_tensor("stream", [128, NPOS * D], F32))
    x1fm = e(nc.sbuf_tensor("x1fm", [128, DCH * TOK], BF16))
    qidx_sb = e(nc.sbuf_tensor("qidx_sb", [128, NPOS], F32))
    partidx = e(nc.sbuf_tensor("partidx", [128, 1], F32))
    identb = e(nc.sbuf_tensor("identb", [128, 128], BF16))
    identf = e(nc.sbuf_tensor("identf", [128, 128], F32))
    onesb = e(nc.sbuf_tensor("onesb", [1, 128], BF16))
    epsb = e(nc.sbuf_tensor("epsb", [128, 1], F32))
    neg8 = e(nc.sbuf_tensor("neg8", [128, 1], F32))
    bn6 = e(nc.sbuf_tensor("bn6", [128, 24], F32))
    mvall = e(nc.sbuf_tensor("mvall", [128, 2 * NPOS], F32))
    lnv = e(nc.sbuf_tensor("lnv", [128, NPOS], F32))
    rstd = e(nc.sbuf_tensor("rstd", [128, NPOS], F32))
    w1sh = e(nc.sbuf_tensor("w1sh", [128, DCH * FE], BF16))

    psA = e(nc.psum_tensor("psA", [128, 512], F32))
    psB = e(nc.psum_tensor("psB", [128, 512], F32))
    psC = e(nc.psum_tensor("psC", [128, 512], F32))
    psD = e(nc.psum_tensor("psD", [128, 512], F32))
    psT1 = e(nc.psum_tensor("psT1", [128, 1024], BF16))
    psT2 = e(nc.psum_tensor("psT2", [128, 1024], BF16))
    psO = e(nc.psum_tensor("psO", [128, 512], F32))
    psE = e(nc.psum_tensor("psE", [128, 512], F32))
    psT = [psT1, psT2]
    psH = [psA, psB, psE]

    dS = Sem(nc, "dS")
    dWx = Sem(nc, "dWx")
    dK = [Sem(nc, f"dK{i}") for i in range(3)]
    dW1 = [Sem(nc, f"dW1_{i}") for i in range(2)]
    dW2 = [Sem(nc, f"dW2_{i}") for i in range(2)]
    dPR = Sem(nc, "dPR")
    dWS = Sem(nc, "dWS")
    dQI = Sem(nc, "dQI")
    cT = Sem(nc, "cT")
    cV = Sem(nc, "cV")
    cS = Sem(nc, "cS")
    cG = Sem(nc, "cG")

    actx = ExitStack()
    a = actx.enter_context
    colbuf = a(nc.sbuf_tensor("colbuf", [128, MSPAN * 128], F32))
    kfm = a(nc.sbuf_tensor("kfm", [64, S], BF16))
    xv1tm = a(nc.sbuf_tensor("xv1tm", [128, NKT * (KD + 1)], BF16))
    qpos = a(nc.sbuf_tensor("qpos", [64, NPOS * 128], BF16))
    PBW = [max(E_PROG[p] for p in range(NPOS) if p % 4 == r) for r in range(4)]
    PB = [a(nc.sbuf_tensor(f"pbuf{r}", [128, PBW[r] * 128], BF16)) for r in range(4)]
    qrowb = a(nc.sbuf_tensor("qrowb", [128, NPOS * 128], F32))
    qT_sb = a(nc.sbuf_tensor("qT_sb", [16, 128], F32))
    kcols = a(nc.sbuf_tensor("kcols", [128, NPOS * MSPAN], F32))
    lrow = [a(nc.sbuf_tensor(f"lrow{i}", [1, 128], F32)) for i in range(2)]
    ones1f = a(nc.sbuf_tensor("ones1f", [1, 1], F32))
    masks_all = a(nc.sbuf_tensor("masks_all", [128, 8 * MSPAN * 128], BF16))
    lnl = a(nc.sbuf_tensor("lnl", [128, NPOS], F32))
    rl_col = a(nc.sbuf_tensor("rl_col", [128, NPOS], F32))
    ob = a(nc.sbuf_tensor("ob", [64, NPOS * 128], BF16))
    wqb = a(nc.sbuf_tensor("wqb", [128, DCH * KD], BF16))
    wkb = a(nc.sbuf_tensor("wkb", [128, DCH * KD], BF16))
    wv1b = a(nc.sbuf_tensor("wv1b", [128, DCH * KD], BF16))
    wv2b = a(nc.sbuf_tensor("wv2b", [64, D], BF16))
    x1btm = [a(nc.sbuf_tensor(f"x1btm{i}", [128, D], BF16)) for i in range(2)]

    ictx = ExitStack()
    i_ = ictx.enter_context
    x0fm = [i_(nc.sbuf_tensor(f"x0fm{i}", [128, DCH * 512], BF16)) for i in range(3)]
    xv1w = [i_(nc.sbuf_tensor(f"xv1w{i}", [64, 512], BF16)) for i in range(2)]

    P = Phase(nc)
    gq, tq, vq, sq, syq = [], [], [], [], []

    def pA_sync(sync):
        sync.dma_start(out=qidx_sb[:, :], in_=qidx[:, :]).then_inc(dQI.h, 16)
        for g in range(4):
            sync.dma_start(
                out=bass.AP(stream, g * 4 * D, [[NPOS * D, 128], [D, 4], [1, D]]),
                in_=bass.AP(x_own.tensor, g * 4 * 128 * D, [[D, 128], [128 * D, 4], [1, D]]),
            ).then_inc(dS.h, 16)
    syq.append(pA_sync)
    dQI.n += 16
    MK_QI = dQI.n
    dS.n += 16 * 4
    MK_IN = dS.n

    def pA_gp0(g):
        g.iota(colbuf[:, :], [[1, MSPAN * 128]], channel_multiplier=0,
               allow_small_or_imprecise_dtypes=True)
        g.iota(partidx[:, :], [[0, 1]], channel_multiplier=1,
               allow_small_or_imprecise_dtypes=True)
        g.memset(onesb[:, :], 1.0)
        g.memset(epsb[:, :], 1e-5)
        g.memset(neg8[:, :], -8.0)
        g.memset(xv1tm[:, :], 1.0)
        g.memset(ones1f[:, :], 1.0)
        for p_ in range(NPOS):
            g.iota(kcols[:, p_ * MSPAN:(p_ + 1) * MSPAN], [[128, MSPAN]],
                   base=MASK_LO[p_] * 128, channel_multiplier=1,
                   allow_small_or_imprecise_dtypes=True)
        for (dst, src) in ((wqb, w_q), (wkb, w_k), (wv1b, w_v1)):
            g.dma_start(out=dst[:, :], in_=src[:, :]).then_inc(dWx.h, 16)
        g.dma_start(out=wv2b[:, :], in_=w_v2[:, :]).then_inc(dWx.h, 16)
        g.drain()
        g.sem_inc(cG.h, 1)
    gq.append(pA_gp0)

    def emit_w1sh(g):
        g.dma_start(out=w1sh[:, :], in_=Ws1[:, :]).then_inc(dWS.h, 16)
    dWx.n += 16 * 4
    MK_WATT = dWx.n
    cG.n += 1
    dWS.n += 16
    MK_WS1 = dWS.n

    def pA_scalar0(s):
        s.add_instruction(mybir.InstLoadActFuncSet(
            name=nc.get_next_instruction_name(), act_func_set_id=6, ins=[], outs=[]))
    sq.append(pA_scalar0)

    def pA_consts(v):
        v.wait_ge(cG.h, 1)
        v.tensor_scalar(identb[:, :], colbuf[:, 0:128], partidx[:, :], None, Alu.is_equal)
        v.tensor_scalar(identf[:, :], colbuf[:, 0:128], partidx[:, :], None, Alu.is_equal)
        v.drain()
        v.sem_inc(cV.h, 1)
    vq.append(pA_consts)
    cV.n += 1
    MK_IDENT = cV.n
    tq.append(lambda t, m=MK_IDENT: t.wait_ge(cV.h, m))
    tq.append(lambda t, m=MK_WATT: t.wait_ge(dWx.h, m))

    psS_free = {0: (cV.h, 0), 1: (cV.h, 0), 2: (cV.h, 0)}
    psTf = {0: 0, 1: 0}
    kfm_mark = [0] * 8
    qpos_mark = [0] * 4
    xv1tm_mark = [0] * 8
    xv1w_guard = [0, 0]
    wv_proj_done = [0] * 12
    wv_dma = [0] * 12
    wno_ctr = [0]
    pbank = [0]

    def emit_wave(kind, wi):
        wno = wno_ctr[0]
        wno_ctr[0] += 1
        par = wno % 3
        if wno >= 3:
            gq.append(lambda g, m=wv_proj_done[wno - 3]: g.wait_ge(cT.h, m))

        def wdma(g, kind=kind, wi=wi, par=par):
            if kind == "own":
                src_t, pstep = x_ownT.tensor, 4 * DCH * 512
            else:
                src_t, pstep = x_keysT.tensor, 8 * DCH * 512
            g.dma_start(
                out=bass.AP(x0fm[par], 0, [[DCH * 512, 128], [1, DCH * 512]]),
                in_=bass.AP(src_t, wi * DCH * 512, [[pstep, 128], [1, DCH * 512]]),
            ).then_inc(dK[par].h, 16)
        gq.append(wdma)
        dK[par].n += 16
        wv_dma[wno] = dK[par].n
        tq.append(lambda t, m=wv_dma[wno], sh=dK[par].h: t.wait_ge(sh, m))
        projs = ((wqb, "q"),) if kind == "own" else ((wkb, "k"), (wv1b, "v"))
        for (wb, which) in projs:
            bank = pbank[0]
            pbank[0] ^= 1
            tq.append(lambda t, hv=psS_free[bank]: t.wait_ge(hv[0], hv[1]))
            for ch in range(DCH):
                def mm(t, wb=wb, ch=ch, par=par, bank=bank, st=(ch == 0), sp=(ch == DCH - 1)):
                    ins = t.matmul(psH[bank][0:64, 0:512],
                                   wb[:, ch * KD:(ch + 1) * KD],
                                   x0fm[par][:, ch * 512:(ch + 1) * 512],
                                   start=st, stop=sp)
                    if sp:
                        ins.then_inc(cT.h, 1)
                tq.append(mm)
            cT.n += 1
            mmark = cT.n
            if which == "q":
                sq.append(lambda s, m=mmark: s.wait_ge(cT.h, m))

                def qev(s, wi=wi, bank=bank):
                    s.activation(qpos[:, wi * 512:(wi + 1) * 512],
                                 psH[bank][0:64, 0:512], Act.Copy,
                                 bias=0.0, scale=0.125).then_inc(cS.h, 1)
                sq.append(qev)
                cS.n += 1
                qpos_mark[wi] = cS.n
                psS_free[bank] = (cS.h, cS.n)
            elif which == "k":
                vq.append(lambda v, m=mmark: v.wait_ge(cT.h, m))

                def kev(v, wi=wi, bank=bank):
                    v.tensor_copy(kfm[:, wi * 512:(wi + 1) * 512],
                                  psH[bank][0:64, 0:512]).then_inc(cV.h, 1)
                vq.append(kev)
                cV.n += 1
                kfm_mark[wi] = cV.n
                psS_free[bank] = (cV.h, cV.n)
            else:
                vq.append(lambda v, m=mmark: v.wait_ge(cT.h, m))
                slot = wi % 2

                def vev(v, slot=slot, bank=bank):
                    v.tensor_copy(xv1w[slot][:, :],
                                  psH[bank][0:64, 0:512]).then_inc(cV.h, 1)
                vq.append(vev)
                cV.n += 1
                vev_mark = cV.n
                psS_free[bank] = (cV.h, cV.n)

                def emit_trv(wi=wi, slot=slot, vm=vev_mark):
                    tb = wi % 2
                    tq.append(lambda t, m=vm: t.wait_ge(cV.h, m))
                    tq.append(lambda t, m=psTf[tb]: t.wait_ge(cV.h, m))
                    for kb in range(4):
                        def trv(t, kb=kb, slot=slot, tb=tb, last=(kb == 3)):
                            ins = t.transpose(psT[tb][0:128, kb * KD:(kb + 1) * KD],
                                              xv1w[slot][:, kb * 128:(kb + 1) * 128],
                                              identb[0:64, 0:64])
                            if last:
                                ins.then_inc(cT.h, 1)
                        tq.append(trv)
                    cT.n += 1
                    xv1w_guard[slot] = cT.n
                    vq.append(lambda v, m=cT.n: v.wait_ge(cT.h, m))

                    def xev(v, wi=wi, tb=tb):
                        v.tensor_copy(
                            bass.AP(xv1tm, wi * 4 * (KD + 1),
                                    [[NKT * (KD + 1), 128], [KD + 1, 4], [1, KD]]),
                            psT[tb][:, 0:256]).then_inc(cV.h, 1)
                    vq.append(xev)
                    cV.n += 1
                    xv1tm_mark[wi] = cV.n
                    psTf[tb] = cV.n
                defer_q.append([2, emit_trv])
        wv_proj_done[wno] = cT.n

    psO_free = [0, 0]
    psCD_free = [0]
    sbank_rr = [0]
    smm_mark = [[0] * 8 for _ in range(NPOS)]
    maxm = [[0] * 8 for _ in range(NPOS)]
    resc_mark = [0] * NPOS
    expm = [0] * NPOS
    ltr_mark = [0] * NPOS
    lcol_free = [0, 0]
    pvm = [0] * NPOS
    oev_mark = [0] * NPOS
    attnm = [0] * NPOS
    aev_mark = [0] * NPOS

    def chunks_of(E):
        out = []
        b0 = 0
        while b0 < E:
            blks = min(4, E - b0)
            out.append((b0, blks))
            b0 += blks
        return out

    exp_cm = [[0] * 8 for _ in range(NPOS)]

    rl_mark = [0] * NPOS

    def emit_head(p):
        E = E_PROG[p]
        mlo = MASK_LO[p]
        cl = chunks_of(E)
        mb0 = (p % 8) * MSPAN * 128
        tq.append(lambda t, m=qpos_mark[p // 4]: t.wait_ge(cS.h, m))
        for ci, (cb, blks) in enumerate(cl):
            bank = sbank_rr[0] % 3
            sbank_rr[0] += 1
            tq.append(lambda t, hv=psS_free[bank]: t.wait_ge(hv[0], hv[1]))
            kw = (cb + blks - 1) // 4
            tq.append(lambda t, m=kfm_mark[kw]: t.wait_ge(cV.h, m))

            def smm(t, p=p, cb=cb, blks=blks, bank=bank):
                for j in range(blks):
                    ins = t.matmul(psH[bank][:, j * 128:(j + 1) * 128],
                                   kfm[:, (cb + j) * 128:(cb + j + 1) * 128],
                                   qpos[:, p * 128:(p + 1) * 128],
                                   start=True, stop=True)
                    if j == blks - 1:
                        ins.then_inc(cT.h, 1)
            tq.append(smm)
            cT.n += 1
            smm_mark[p][ci] = cT.n
            mstart = max(cb, mlo)
            mend = cb + blks
            if mstart < mend:
                vq.append(lambda v, m=smm_mark[p][ci]: v.wait_ge(cT.h, m))
                vq.append(lambda v, m=mask_mark[p]: v.wait_ge(cV.h, m))

                def vmsk(v, cb=cb, bank=bank, mstart=mstart, mend=mend, mlo=mlo, mb0=mb0):
                    v.tensor_tensor(
                        psH[bank][:, (mstart - cb) * 128:(mend - cb) * 128],
                        psH[bank][:, (mstart - cb) * 128:(mend - cb) * 128],
                        masks_all[:, mb0 + (mstart - mlo) * 128: mb0 + (mend - mlo) * 128],
                        Alu.add)
                    v.drain()
                    v.sem_inc(cV.h, 1)
                vq.append(vmsk)
                cV.n += 1
                sq.append(lambda s, m=cV.n: s.wait_ge(cV.h, m))
            else:
                sq.append(lambda s, m=smm_mark[p][ci]: s.wait_ge(cT.h, m))
            if ci == 0 and p + 4 < NPOS:
                sq.append(lambda s, m=pvm[p + 4]: s.wait_ge(cT.h, m))

            def sexp(s, p=p, ci=ci, cb=cb, blks=blks, bank=bank):
                s.activation(PB[p % 4][:, cb * 128:(cb + blks) * 128],
                             psH[bank][:, 0:blks * 128], Act.Exp,
                             bias=neg8[:, :], scale=1.0).then_inc(cS.h, 1)
            sq.append(sexp)
            cS.n += 1
            exp_cm[p][ci] = cS.n
            psS_free[bank] = (cS.h, cS.n)
        expm[p] = cS.n

    def emit_tail(p):
        E = E_PROG[p]
        so = (p % 2) * 128
        lc = 256 + (p % 2)
        tq.append(lambda t, m=psO_free[p % 2]: t.wait_ge(cS.h, m))
        tq.append(lambda t, m=lcol_free[p % 2]: t.wait_ge(cS.h, m))
        if p + 1 < NPOS:
            tq.append(lambda t, m=rl_mark[p + 1]: t.wait_ge(cS.h, m))
        tq.append(lambda t, m=xv1tm_mark[(E - 1) // 4]: t.wait_ge(cV.h, m))
        nbat = (E + 7) // 8
        for bi in range(nbat):
            lo, hi = bi * 8, min(bi * 8 + 8, E)
            tq.append(lambda t, m=exp_cm[p][(hi - 1) // 4]: t.wait_ge(cS.h, m))
            for kb in range(lo, hi):
                def pv(t, kb=kb, E=E, so=so, p=p):
                    ins = t.matmul(psO[0:KD + 1, so:so + 128],
                                   xv1tm[:, kb * (KD + 1):(kb + 1) * (KD + 1)],
                                   PB[p % 4][:, kb * 128:(kb + 1) * 128],
                                   start=(kb == 0), stop=(kb == E - 1))
                    if kb == E - 1:
                        ins.then_inc(cT.h, 1)
                tq.append(pv)
        cT.n += 1
        pvm[p] = cT.n
        sq.append(lambda s, m=pvm[p]: s.wait_ge(cT.h, m))
        if p + 2 < NPOS:
            sq.append(lambda s, m=ltr_mark[p + 2]: s.wait_ge(cT.h, m))

        def oev(s, p=p, so=so):
            s.copy(ob[:, p * 128:(p + 1) * 128], psO[0:64, so:so + 128])
            s.copy(lrow[p % 2][0:1, :], psO[64:65, so:so + 128]).then_inc(cS.h, 1)
        sq.append(oev)
        cS.n += 1
        oev_mark[p] = cS.n
        psO_free[p % 2] = cS.n

    def emit_tail_fin(p):
        so = (p % 2) * 128
        lc = 256 + (p % 2)
        tq.append(lambda t, m=oev_mark[p]: t.wait_ge(cS.h, m))

        def ltr(t, p=p, lc=lc):
            t.matmul(psO[:, lc:lc + 1], lrow[p % 2][0:1, :], ones1f[0:1, 0:1],
                     start=True, stop=True).then_inc(cT.h, 1)
        tq.append(ltr)
        cT.n += 1
        ltr_mark[p] = cT.n
        sq.append(lambda s, m=ltr_mark[p]: s.wait_ge(cT.h, m))

        def srl(s, p=p, lc=lc):
            s.activation(lnl[:, p:p + 1], psO[:, lc:lc + 1], Act.Ln,
                         bias=0.0, scale=1.0)
            s.drain()
            s.activation(rl_col[:, p:p + 1], lnl[:, p:p + 1], Act.Exp,
                         bias=0.0, scale=-1.0).then_inc(cS.h, 1)
        sq.append(srl)
        cS.n += 1
        rl_mark[p] = cS.n
        lcol_free[p % 2] = cS.n
        tq.append(lambda t, m=oev_mark[p]: t.wait_ge(cS.h, m))
        tq.append(lambda t, m=psCD_free[0]: t.wait_ge(cV.h, m))

        def amm(t, p=p):
            t.matmul(psC[:, :], ob[:, p * 128:(p + 1) * 128], wv2b[:, 0:512],
                     start=True, stop=True)
            t.matmul(psD[:, :], ob[:, p * 128:(p + 1) * 128], wv2b[:, 512:1024],
                     start=True, stop=True).then_inc(cT.h, 1)
        tq.append(amm)
        cT.n += 1
        attnm[p] = cT.n
        vq.append(lambda v, m=attnm[p]: v.wait_ge(cT.h, m))
        vq.append(lambda v, m=rl_mark[p]: v.wait_ge(cS.h, m))

        def aev(v, p=p):
            v.scalar_tensor_tensor(stream[:, p * D: p * D + 512], psC[:, :],
                                   rl_col[:, p:p + 1], stream[:, p * D: p * D + 512],
                                   Alu.mult, Alu.add)
            v.scalar_tensor_tensor(stream[:, p * D + 512: p * D + 1024], psD[:, :],
                                   rl_col[:, p:p + 1], stream[:, p * D + 512: p * D + 1024],
                                   Alu.mult, Alu.add).then_inc(cV.h, 1)
        vq.append(aev)
        cV.n += 1
        aev_mark[p] = cV.n
        psCD_free[0] = cV.n

    trx_mark = [0] * NPOS
    x1fm_mark = [0] * NPOS
    stats_mark = [0] * NPOS
    cast_mark = [0] * NPOS

    def ln_stats(v, i):
        half = (i % 2) * 12
        v.bn_stats(bn6[:, half: half + 6], stream[:, i * D: i * D + 512])
        v.bn_stats(bn6[:, half + 6: half + 12], stream[:, i * D + 512: i * D + 1024])
        v.drain()
        v.bn_aggr(mvall[:, 2 * i: 2 * i + 2], bn6[:, half: half + 12]).then_inc(cV.h, 1)

    def emit_ln_stats(p):
        vq.append(lambda v, m=aev_mark[p]: v.wait_ge(cV.h, m))
        vq.append(lambda v, p=p: ln_stats(v, p))
        cV.n += 1
        stats_mark[p] = cV.n

    def emit_ln_mid(p):
        sq.append(lambda s, m=stats_mark[p]: s.wait_ge(cV.h, m))

        def srstd(s, p=p):
            s.activation(lnv[:, p:p + 1],
                         bass.AP(mvall, 2 * p + 1, [[2 * NPOS, 128], [1, 1]]),
                         Act.Ln, bias=epsb[:, :], scale=1.0)
            s.drain()
            s.activation(rstd[:, p:p + 1], lnv[:, p:p + 1], Act.Exp,
                         bias=0.0, scale=-0.5).then_inc(cS.h, 1)
        sq.append(srstd)
        cS.n += 1
        rsm = cS.n
        vq.append(lambda v, m=rsm: v.wait_ge(cS.h, m))

        def appl(v, p=p):
            v.tensor_scalar(stream[:, p * D:(p + 1) * D], stream[:, p * D:(p + 1) * D],
                            mvall[:, 2 * p: 2 * p + 1], rstd[:, p: p + 1],
                            Alu.subtract, Alu.mult).then_inc(cV.h, 1)
        vq.append(appl)
        cV.n += 1
        apm = cV.n
        sq.append(lambda s, m=apm: s.wait_ge(cV.h, m))
        if p + 2 < NPOS:
            sq.append(lambda s, m=trx_mark[p + 2]: s.wait_ge(cT.h, m))

        def cast(s, p=p):
            s.activation(x1btm[p % 2][:, :], stream[:, p * D:(p + 1) * D], Act.Copy,
                         bias=0.0, scale=1.0).then_inc(cS.h, 1)
        sq.append(cast)
        cS.n += 1
        cast_mark[p] = cS.n

    def emit_ln_back(p):
        tq.append(lambda t, m=cast_mark[p]: t.wait_ge(cS.h, m))
        tq.append(lambda t, m=psTf[p % 2]: t.wait_ge(cV.h, m))
        for ch in range(DCH):
            def trx(t, p=p, ch=ch, last=(ch == DCH - 1)):
                ins = t.transpose(psT[p % 2][:, ch * 128:(ch + 1) * 128],
                                  x1btm[p % 2][:, ch * 128:(ch + 1) * 128], identb[:, :])
                if last:
                    ins.then_inc(cT.h, 1)
            tq.append(trx)
        cT.n += 1
        trx_mark[p] = cT.n
        vq.append(lambda v, m=trx_mark[p]: v.wait_ge(cT.h, m))

        def evx(v, p=p):
            v.tensor_copy(
                bass.AP(x1fm, p * 128, [[DCH * TOK, 128], [TOK, DCH], [1, 128]]),
                psT[p % 2][:, :]).then_inc(cV.h, 1)
        vq.append(evx)
        cV.n += 1
        psTf[p % 2] = cV.n
        x1fm_mark[p] = cV.n

    vq.append(lambda v, m=MK_IN: v.wait_ge(dS.h, m))

    tq.append(lambda t: t.wait_ge(dQI.h, MK_QI))
    tq.append(lambda t, m=MK_IDENT: t.wait_ge(cV.h, m))

    def qtr(t):
        t.transpose(psC[0:16, 0:128], qidx_sb[:, :], identf[:, :]).then_inc(cT.h, 1)
    tq.append(qtr)
    cT.n += 1
    qtm = cT.n
    vq.append(lambda v, m=qtm: v.wait_ge(cT.h, m))

    def qev_(v):
        v.tensor_copy(qT_sb[:, :], psC[0:16, 0:128]).then_inc(cV.h, 1)
    vq.append(qev_)
    cV.n += 1
    psCD_free[0] = cV.n
    gq.append(lambda g, m=cV.n: g.wait_ge(cV.h, m))

    def qdma(g):
        g.dma_start(out=bass.AP(qT_dram.tensor, 0, [[128, 16], [1, 128]]),
                    in_=qT_sb[:, :]).then_inc(dPR.h, 16)
    gq.append(qdma)
    dPR.n += 16
    gq.append(lambda g, m=dPR.n: g.wait_ge(dPR.h, m))

    def qbc(g):
        g.dma_start(out=qrowb[:, :],
                    in_=bass.AP(qT_dram.tensor, 0, [[0, 128], [1, NPOS * 128]])
                    ).then_inc(dPR.h, 16)
    gq.append(qbc)
    dPR.n += 16
    qrow_mark = dPR.n
    mask_mark = [0] * NPOS

    def emit_gmask(p):
        vq.append(lambda v, m=qrow_mark: v.wait_ge(dPR.h, m))

        def gmask(v, p=p):
            base = (p % 8) * MSPAN * 128
            for j in range(MSPAN):
                ins = v.tensor_scalar(
                    masks_all[:, base + j * 128: base + (j + 1) * 128],
                    qrowb[:, p * 128:(p + 1) * 128],
                    kcols[:, p * MSPAN + j: p * MSPAN + j + 1],
                    NEG, Alu.is_lt, Alu.mult)
                if j == MSPAN - 1:
                    ins.then_inc(cV.h, 1)
        vq.append(gmask)
        cV.n += 1
        mask_mark[p] = cV.n
    SCHED = [("key", 0), ("own", 3), 15, 14, ("key", 1), 13, 12,
             ("own", 2), ("key", 2), 11, 10, ("key", 3), 9, 8,
             ("own", 1), ("key", 4), 7, 6, ("key", 5), 5, 4,
             ("own", 0), ("key", 6), 3, 2, ("key", 7), 1, 0]
    head_fifo, ln_fifo, st2, st3 = [], [], [], []
    defer_q = []

    def flush_defer():
        keep = []
        for item in defer_q:
            item[0] -= 1
            if item[0] <= 0:
                item[1]()
            else:
                keep.append(item)
        defer_q[:] = keep

    def flush_defer_all():
        while defer_q:
            flush_defer()

    def pump_ln():
        lp = ln_fifo.pop(0)
        emit_ln_stats(lp)
        st2.append(lp)
        if len(st2) > 1:
            l2 = st2.pop(0)
            emit_ln_mid(l2)
            st3.append(l2)
            if len(st3) > 1:
                emit_ln_back(st3.pop(0))

    def pump_tail():
        qp = head_fifo.pop(0)
        emit_tail(qp)
        emit_tail_fin(qp)
        ln_fifo.append(qp)
        if len(ln_fifo) > 3:
            pump_ln()

    for evno, ev in enumerate(SCHED):
        flush_defer()
        if isinstance(ev, tuple):
            emit_wave(*ev)
        else:
            qp = None
            if len(head_fifo) >= 3:
                qp = head_fifo.pop(0)
                emit_tail(qp)
            emit_head(ev)
            if ev >= 8:
                defer_q.append([1, lambda p=ev - 8: emit_gmask(p)])
            head_fifo.append(ev)
            if qp is not None:
                emit_tail_fin(qp)
                ln_fifo.append(qp)
                if len(ln_fifo) > 3:
                    pump_ln()
        if evno == 1:
            gq.append(lambda g: emit_w1sh(g))
            for p_ in reversed(range(8, NPOS)):
                emit_gmask(p_)
    flush_defer_all()
    while head_fifo:
        pump_tail()
    while ln_fifo:
        pump_ln()
    while st2:
        l2 = st2.pop(0)
        emit_ln_mid(l2)
        st3.append(l2)
    while st3:
        emit_ln_back(st3.pop(0))

    P.q["sync"].append(lambda sy, q=syq: [f(sy) for f in q])
    P.q["gpsimd"].append(lambda g, q=gq: [f(g) for f in q])
    P.q["tensor"].append(lambda t, q=tq: [f(t) for f in q])
    P.q["vector"].append(lambda v, q=vq: [f(v) for f in q])
    P.q["scalar"].append(lambda s, q=sq: [f(s) for f in q])
    P.run()
    ictx.close()
    actx.close()

    fctx = ExitStack()
    f_ = fctx.enter_context
    wrb = f_(nc.sbuf_tensor("wrb", [128, DCH * NE], BF16))
    w1s = [f_(nc.sbuf_tensor(f"w1s{i}", [128, DCH * FE], BF16)) for i in range(2)]
    w2s = [f_(nc.sbuf_tensor(f"w2s{i}", [128, FCH * D], BF16)) for i in range(2)]
    hr = [f_(nc.sbuf_tensor(f"hr{i}", [128, 512], BF16)) for i in range(2)]
    hq = [f_(nc.sbuf_tensor(f"hq{i}", [128, 512], BF16)) for i in range(2)]
    bcastP = [f_(nc.sbuf_tensor(f"bcast{i}", [128, TOK], BF16)) for i in range(2)]
    probfm_sb = f_(nc.sbuf_tensor("probfm_sb", [8, TOK], BF16))

    rctx = ExitStack()
    r_ = rctx.enter_context
    lfm = r_(nc.sbuf_tensor("lfm", [8, TOK], F32))
    logits_tm = r_(nc.sbuf_tensor("logits_tm", [128, NPOS * NE], F32))
    ltop = r_(nc.sbuf_tensor("ltop", [128, NPOS * NE], F32))
    probtm = r_(nc.sbuf_tensor("probtm", [128, NPOS * NE], F32))
    probtmb = r_(nc.sbuf_tensor("probtmb", [128, NPOS * NE], BF16))
    dbuf = r_(nc.sbuf_tensor("dbuf", [128, NPOS], F32))
    ubuf = r_(nc.sbuf_tensor("ubuf", [128, NPOS], F32))
    abuf = r_(nc.sbuf_tensor("abuf", [128, NPOS], F32))
    rbuf = r_(nc.sbuf_tensor("rbuf", [128, NPOS], F32))
    gbuf = r_(nc.sbuf_tensor("gbuf", [128, NPOS], F32))

    hctx = ExitStack()
    h_ = hctx.enter_context
    H2a = h_(nc.sbuf_tensor("H2a", [128, FCH * TOK], BF16))
    H2b = h_(nc.sbuf_tensor("H2b", [128, FCH * TOK], BF16))

    P = Phase(nc)
    vq, sq, tq, gq, syq = [], [], [], [], []

    NM = NE + 1
    w1_mark = [0] * NM
    w2_mark = [0] * NM

    def load_w1(g, memb):
        g.dma_start(out=w1s[(memb - 1) % 2][:, :],
                    in_=bass.AP(W1.tensor, (memb - 1) * 128 * DCH * FE,
                                [[DCH * FE, 128], [1, DCH * FE]])
                    ).then_inc(dW1[(memb - 1) % 2].h, 16)

    def load_w2(g, memb):
        if memb == 0:
            src_t, off = Ws2.tensor, 0
        else:
            src_t, off = W2.tensor, (memb - 1) * 128 * FCH * D
        g.dma_start(out=w2s[memb % 2][:, :],
                    in_=bass.AP(src_t, off, [[FCH * D, 128], [1, FCH * D]])
                    ).then_inc(dW2[memb % 2].h, 16)

    def sched_w1(memb):
        dW1[(memb - 1) % 2].n += 16
        w1_mark[memb] = dW1[(memb - 1) % 2].n

    def sched_w2(memb):
        dW2[memb % 2].n += 16
        w2_mark[memb] = dW2[memb % 2].n

    def pB_gp(g):
        g.dma_start(out=wrb[:, :], in_=Wr[:, :]).then_inc(dWx.h, 16)
    gq.append(pB_gp)
    dWx.n += 16
    MK_WR = dWx.n
    gq.append(lambda g: load_w2(g, 0))
    sched_w2(0)
    for memb in (1, 2):
        gq.append(lambda g, memb=memb: load_w1(g, memb))
        sched_w1(memb)
    gq.append(lambda g: load_w2(g, 1))
    sched_w2(1)

    def ln_statsB(v, i):
        half = (i % 2) * 12
        v.bn_stats(bn6[:, half: half + 6], stream[:, i * D: i * D + 512])
        v.bn_stats(bn6[:, half + 6: half + 12], stream[:, i * D + 512: i * D + 1024])
        v.drain()
        v.bn_aggr(mvall[:, 2 * i: 2 * i + 2], bn6[:, half: half + 12]).then_inc(cV.h, 1)

    psHf = {0: (cV.h, 0), 1: (cV.h, 0)}
    hq_free = {0: 0, 1: 0}
    psCD2 = {0: 0, 1: 0}
    h2_guard = {id(H2a): 0, id(H2b): 0}
    bcast_free = {0: 0, 1: 0}
    mm1_done = [0] * NM
    mm2_done = [0] * NM
    prow_mark = [0] * NM
    probfm_mark = [0]
    resid2_mark = [0] * NPOS

    prob_st = [0]

    def gen_bcast(memb):
        buf = bcastP[memb % 2]
        if prob_st[0] == 0:
            gq.append(lambda g, m=probfm_mark[0]: g.wait_ge(cV.h, m))

            def pst(g):
                g.dma_start(out=bass.AP(prob_dram.tensor, 0, [[TOK, 8], [1, TOK]]),
                            in_=probfm_sb[:, :]).then_inc(dPR.h, 16)
            gq.append(pst)
            dPR.n += 16
            prob_st[0] = dPR.n
        gq.append(lambda g, m=dPR.n: g.wait_ge(dPR.h, m))
        gq.append(lambda g, m=bcast_free[memb % 2]: g.wait_ge(cV.h, m))

        def bdma(g, memb=memb, buf=buf):
            g.dma_start(out=buf[:, :],
                        in_=bass.AP(prob_dram.tensor, (memb - 1) * TOK,
                                    [[0, 128], [1, TOK]])).then_inc(dPR.h, 16)
        gq.append(bdma)
        dPR.n += 16
        prow_mark[memb] = dPR.n

    def mm1(memb, H2x):
        h2t4 = [None] * 4
        if memb == 0:
            tq.append(lambda t: t.wait_ge(dWS.h, MK_WS1))
            w1buf = w1sh
        else:
            tq.append(lambda t, m=w1_mark[memb], sh=dW1[(memb - 1) % 2].h: t.wait_ge(sh, m))
            w1buf = w1s[(memb - 1) % 2]
        tq.append(lambda t, m=h2_guard[id(H2x)]: t.wait_ge(cT.h, m))
        for t4 in range(4):
            for fc in range(FCH):
                idx = t4 * FCH + fc
                bank = idx % 2
                tq.append(lambda t, hv=psHf[bank]: t.wait_ge(hv[0], hv[1]))
                for ch in range(DCH):
                    def hmm(t, fc=fc, t4=t4, ch=ch, bank=bank, w1buf=w1buf,
                            st=(ch == 0), sp=(ch == DCH - 1)):
                        ins = t.matmul(psH[bank][:, :],
                                       w1buf[:, ch * FE + fc * 128: ch * FE + (fc + 1) * 128],
                                       x1fm[:, ch * TOK + t4 * 512: ch * TOK + (t4 + 1) * 512],
                                       start=st, stop=sp)
                        if sp:
                            ins.then_inc(cT.h, 1)
                    tq.append(hmm)
                cT.n += 1
                gmark = cT.n
                hslot = idx % 2
                sq.append(lambda s, m=gmark: s.wait_ge(cT.h, m))

                def hrel(s, bank=bank, hslot=hslot):
                    s.activation(hr[hslot][:, :], psH[bank][:, :], Act.Relu,
                                 bias=0.0, scale=1.0).then_inc(cS.h, 1)
                sq.append(hrel)
                cS.n += 1
                psHf[bank] = (cS.h, cS.n)
                if memb != 0:
                    sq.append(lambda s, m=hq_free[hslot]: s.wait_ge(cV.h, m))

                def hsqr(s, fc=fc, t4=t4, hslot=hslot, H2x=H2x, memb=memb):
                    s.drain()
                    if memb == 0:
                        dst = H2x[:, fc * TOK + t4 * 512: fc * TOK + (t4 + 1) * 512]
                    else:
                        dst = hq[hslot][:, :]
                    s.activation(dst, hr[hslot][:, :], Act.Square,
                                 bias=0.0, scale=1.0).then_inc(cS.h, 1)
                sq.append(hsqr)
                cS.n += 1
                sqm = cS.n
                if memb != 0:
                    vq.append(lambda v, m=sqm: v.wait_ge(cS.h, m))
                    if idx == 0:
                        vq.append(lambda v, m=prow_mark[memb], h=dPR.h:
                                  v.wait_ge(h, m))

                    def hbc(v, fc=fc, t4=t4, hslot=hslot, H2x=H2x, memb=memb):
                        v.tensor_tensor(
                            H2x[:, fc * TOK + t4 * 512: fc * TOK + (t4 + 1) * 512],
                            hq[hslot][:, :],
                            bcastP[memb % 2][:, t4 * 512:(t4 + 1) * 512],
                            Alu.mult).then_inc(cV.h, 1)
                    vq.append(hbc)
                    cV.n += 1
                    hq_free[hslot] = cV.n
            if memb == 0:
                h2t4[t4] = (cS.h, cS.n)
            else:
                h2t4[t4] = (cV.h, cV.n)
        if memb != 0:
            bcast_free[memb % 2] = cV.n
        mm1_done[memb] = cT.n
        return h2t4

    st_marks = [0] * NPOS

    def emit_ln2_store(tt):
        sq.append(lambda s, m=resid2_mark[tt]: s.wait_ge(cV.h, m))

        def rstd2(s, tt=tt):
            s.activation(lnv[:, tt:tt + 1],
                         bass.AP(mvall, 2 * tt + 1, [[2 * NPOS, 128], [1, 1]]),
                         Act.Ln, bias=epsb[:, :], scale=1.0)
            s.drain()
            s.activation(rstd[:, tt:tt + 1], lnv[:, tt:tt + 1], Act.Exp,
                         bias=0.0, scale=-0.5).then_inc(cS.h, 1)
        sq.append(rstd2)
        cS.n += 1
        rm = cS.n
        vq.append(lambda v, m=rm: v.wait_ge(cS.h, m))

        def appl2(v, tt=tt):
            v.tensor_scalar(stream[:, tt * D:(tt + 1) * D], stream[:, tt * D:(tt + 1) * D],
                            mvall[:, 2 * tt: 2 * tt + 1], rstd[:, tt: tt + 1],
                            Alu.subtract, Alu.mult).then_inc(cV.h, 1)
        vq.append(appl2)
        cV.n += 1
        st_marks[tt] = cV.n
        syq.append(lambda sy, m=st_marks[tt]: sy.wait_ge(cV.h, m))

        def sto(sy, tt=tt):
            sy.dma_start(out=bass.AP(y.tensor, tt * 128 * D, [[D, 128], [1, D]]),
                         in_=stream[:, tt * D:(tt + 1) * D]).then_inc(dS.h, 16)
        syq.append(sto)
        dS.n += 16

    def mm2(members, h2s, h2marks, final=False):
        for memb in members:
            tq.append(lambda t, m=w2_mark[memb], sh=dW2[memb % 2].h: t.wait_ge(sh, m))
        nm = len(members)
        for tt in range(NPOS):
            if tt % 4 == 0:
                for hm in h2marks:
                    tq.append(lambda t, hv=hm[tt // 4]: t.wait_ge(hv[0], hv[1]))
            for half in range(2):
                ps = psC if half == 0 else psD
                tq.append(lambda t, m=psCD2[half]: t.wait_ge(cV.h, m))
                for mi in range(nm):
                    H2x = h2s[mi]
                    w2buf = w2s[members[mi] % 2]
                    for c in range(FCH):
                        def omm(t, tt=tt, half=half, c=c, H2x=H2x, w2buf=w2buf, ps=ps,
                                st=(mi == 0 and c == 0), sp=(mi == nm - 1 and c == FCH - 1)):
                            ins = t.matmul(ps[:, :],
                                           H2x[:, c * TOK + tt * 128: c * TOK + (tt + 1) * 128],
                                           w2buf[:, c * D + half * 512: c * D + half * 512 + 512],
                                           start=st, stop=sp)
                            if sp:
                                ins.then_inc(cT.h, 1)
                        tq.append(omm)
                cT.n += 1
                vq.append(lambda v, m=cT.n: v.wait_ge(cT.h, m))

                def oev2(v, tt=tt, half=half, ps=ps):
                    v.tensor_tensor(stream[:, tt * D + half * 512: tt * D + half * 512 + 512],
                                    stream[:, tt * D + half * 512: tt * D + half * 512 + 512],
                                    ps[:, :], Alu.add).then_inc(cV.h, 1)
                vq.append(oev2)
                cV.n += 1
                psCD2[half] = cV.n
            if final:
                vq.append(lambda v: v.drain())
                vq.append(lambda v, tt=tt: ln_statsB(v, tt))
                cV.n += 1
                resid2_mark[tt] = cV.n
                if tt > 0:
                    emit_ln2_store(tt - 1)
        if final:
            emit_ln2_store(NPOS - 1)
        for memb in members:
            mm2_done[memb] = cT.n
        return cT.n

    h2a_marks = mm1(0, H2a)
    tq.append(lambda t: t.wait_ge(dWx.h, MK_WR))
    for j in range(4):
        bank = j % 2
        tq.append(lambda t, hv=psHf[bank]: t.wait_ge(hv[0], hv[1]))
        for ch in range(DCH):
            def lmm(t, ch=ch, j=j, bank=bank, st=(ch == 0), sp=(ch == DCH - 1)):
                ins = t.matmul(psH[bank][0:8, 0:512], wrb[:, ch * NE:(ch + 1) * NE],
                               x1fm[:, ch * TOK + j * 512: ch * TOK + j * 512 + 512],
                               start=st, stop=sp)
                if sp:
                    ins.then_inc(cT.h, 1)
            tq.append(lmm)
        cT.n += 1
        vq.append(lambda v, m=cT.n: v.wait_ge(cT.h, m))

        def lev(v, j=j, bank=bank):
            v.tensor_copy(lfm[:, j * 512:(j + 1) * 512], psH[bank][0:8, 0:512]
                          ).then_inc(cV.h, 1)
        vq.append(lev)
        cV.n += 1
        psHf[bank] = (cV.h, cV.n)
    lev_mark = cV.n
    done = mm2([0], [H2a], [h2a_marks])
    tq.append(lambda t, m=lev_mark: t.wait_ge(cV.h, m))
    for i in range(NPOS):
        def ltr(t, i=i, last=(i == NPOS - 1)):
            ins = t.transpose(psE[:, i * 8:(i + 1) * 8], lfm[:, i * 128:(i + 1) * 128],
                              identf[0:8, 0:8])
            if last:
                ins.then_inc(cT.h, 1)
        tq.append(ltr)
    cT.n += 1
    vq.append(lambda v, m=cT.n: v.wait_ge(cT.h, m))

    def p5_prob(v):
        v.tensor_copy(logits_tm[:, :], psE[:, 0:NPOS * NE])
        v.drain()
        for i in range(NPOS):
            v.max(ltop[:, i * 8:(i + 1) * 8], logits_tm[:, i * 8:(i + 1) * 8])
        v.drain()
        v.tensor_tensor(dbuf[:, :], bass.AP(ltop, 1, [[NPOS * NE, 128], [NE, NPOS]]),
                        bass.AP(ltop, 0, [[NPOS * NE, 128], [NE, NPOS]]), Alu.subtract)
        v.drain()
        v.sem_inc(cV.h, 1)
    vq.append(p5_prob)
    cV.n += 1
    sq.append(lambda s, m=cV.n: s.wait_ge(cV.h, m))

    def p5_exp(s):
        s.activation(ubuf[:, :], dbuf[:, :], Act.Exp, bias=0.0, scale=1.0)
        s.drain()
        s.sem_inc(cS.h, 1)
    sq.append(p5_exp)
    cS.n += 1
    vq.append(lambda v, m=cS.n: v.wait_ge(cS.h, m))

    def p5_gate(v):
        v.tensor_scalar(abuf[:, :], ubuf[:, :], 1.0, None, Alu.add)
        v.drain()
        v.reciprocal(rbuf[:, :], abuf[:, :])
        v.drain()
        v.tensor_scalar(gbuf[:, :], rbuf[:, :], 1.0, None, Alu.is_ge)
        v.drain()
        for i in range(NPOS):
            v.tensor_scalar(probtm[:, i * 8:(i + 1) * 8], logits_tm[:, i * 8:(i + 1) * 8],
                            ltop[:, i * 8:i * 8 + 1], gbuf[:, i:i + 1],
                            Alu.is_equal, Alu.mult)
        v.drain()
        v.tensor_copy(probtmb[:, :], probtm[:, :])
        v.drain()
        v.sem_inc(cV.h, 1)
    vq.append(p5_gate)
    cV.n += 1
    tq.append(lambda t, m=cV.n: t.wait_ge(cV.h, m))
    psTfB = {0: 0, 1: 0}
    for bat in range(2):
        tq.append(lambda t, m=psTfB[bat]: t.wait_ge(cV.h, m))
        for i8 in range(8):
            i = bat * 8 + i8
            def ptr2(t, i=i, i8=i8, bat=bat, last=(i8 == 7)):
                ins = t.transpose(psT[bat][0:8, i8 * 128:(i8 + 1) * 128],
                                  probtmb[:, i * 8:(i + 1) * 8], identb[:, :])
                if last:
                    ins.then_inc(cT.h, 1)
            tq.append(ptr2)
        cT.n += 1
        vq.append(lambda v, m=cT.n: v.wait_ge(cT.h, m))

        def pev2(v, bat=bat):
            v.tensor_copy(probfm_sb[:, bat * 1024:(bat + 1) * 1024],
                          psT[bat][0:8, 0:1024]).then_inc(cV.h, 1)
        vq.append(pev2)
        cV.n += 1
        psTfB[bat] = cV.n
    probfm_mark[0] = cV.n
    h2_guard[id(H2a)] = done
    gq.append(lambda g, m=done: g.wait_ge(cT.h, m))
    gq.append(lambda g: load_w2(g, 2))
    sched_w2(2)


    gen_bcast(1)
    gen_bcast(2)
    for g_ in range(4):
        ma, mb = 2 * g_ + 1, 2 * g_ + 2
        bufa = H2b if ma % 2 == 1 else H2a
        bufb = H2b if mb % 2 == 1 else H2a
        ma_marks = mm1(ma, bufa)
        if ma + 2 < NM:
            gq.append(lambda g, m=mm1_done[ma]: g.wait_ge(cT.h, m))
            gq.append(lambda g, memb=ma + 2: load_w1(g, memb))
            sched_w1(ma + 2)
            gen_bcast(ma + 2)
        mb_marks = mm1(mb, bufb)
        if mb + 2 < NM:
            gen_bcast(mb + 2)
        if mb + 2 < NM:
            gq.append(lambda g, m=mm1_done[mb]: g.wait_ge(cT.h, m))
            gq.append(lambda g, memb=mb + 2: load_w1(g, memb))
            sched_w1(mb + 2)
        done = mm2([ma, mb], [bufa, bufb], [ma_marks, mb_marks], final=(g_ == 3))
        h2_guard[id(H2a)] = done
        h2_guard[id(H2b)] = done
        for nxt in (ma + 2, mb + 2):
            if nxt < NM:
                gq.append(lambda g, m=done: g.wait_ge(cT.h, m))
                gq.append(lambda g, memb=nxt: load_w2(g, memb))
                sched_w2(nxt)

    syq.append(lambda sy, m=dS.n: sy.wait_ge(dS.h, m))

    P.q["gpsimd"].append(lambda g, q=gq: [f(g) for f in q])
    P.q["vector"].append(lambda v, q=vq: [f(v) for f in q])
    P.q["scalar"].append(lambda s, q=sq: [f(s) for f in q])
    P.q["tensor"].append(lambda t, q=tq: [f(t) for f in q])
    P.q["sync"].append(lambda sy, q=syq: [f(sy) for f in q])
    P.run()

    hctx.close()
    rctx.close()
    fctx.close()
    ctx.close()
    return nc


_NC_CACHE = None


def _get_nc():
    global _NC_CACHE
    if _NC_CACHE is None:
        _NC_CACHE = build_nc()
    return _NC_CACHE


def kernel(L0_attn_x, L0_attn_w_q, L0_attn_w_k, L0_attn_w_v1, L0_attn_w_v2,
           L0_ffn_Ws1, L0_ffn_Ws2, L0_ffn_W1, L0_ffn_W2, L0_ffn_Wr):
    x = np.ascontiguousarray(np.asarray(L0_attn_x, dtype=np.float32))
    def pm(a, nch):
        a = np.asarray(a, np.float32)
        return np.ascontiguousarray(
            a.reshape(nch, 128, a.shape[-1]).transpose(1, 0, 2).reshape(128, -1))

    w = {
        "w_q": pm(L0_attn_w_q, DCH),
        "w_k": pm(L0_attn_w_k, DCH),
        "w_v1": pm(L0_attn_w_v1, DCH),
        "w_v2": np.ascontiguousarray(np.asarray(L0_attn_w_v2, np.float32)),
        "Ws1": pm(L0_ffn_Ws1, DCH),
        "Ws2": pm(L0_ffn_Ws2, FCH),
        "W1": np.ascontiguousarray(np.stack(
            [pm(np.asarray(L0_ffn_W1, np.float32)[e], DCH) for e in range(NE)])),
        "W2": np.ascontiguousarray(np.stack(
            [pm(np.asarray(L0_ffn_W2, np.float32)[e], FCH) for e in range(NE)])),
        "Wr": pm(L0_ffn_Wr, DCH),
    }
    nc = _get_nc()
    xW = [np.ascontiguousarray(
        x[b].reshape(8, 512, 8, 128).transpose(3, 0, 2, 1).reshape(128, 8 * DCH * 512))
        for b in range(B)]
    in_maps = []
    for core in range(8):
        b, h = core // 2, core % 2
        tiles = TILES[h]
        xb = x[b]
        xo = np.ascontiguousarray(xb.reshape(NKT, 128, D)[tiles].reshape(TOK, D))
        qi = (np.asarray(tiles, np.float32)[None, :] * 128.0
              + np.arange(128, dtype=np.float32)[:, None])
        xoW = np.ascontiguousarray(
            xo.reshape(4, 512, 8, 128).transpose(3, 0, 2, 1).reshape(128, 4 * DCH * 512))
        m = {"x_own": xo, "x_keysT": xW[b], "x_ownT": xoW,
             "qidx": np.ascontiguousarray(qi)}
        m.update(w)
        in_maps.append(m)
    res = run_bass_kernel_spmd(nc, in_maps, core_ids=list(range(8)))
    out = np.empty((B, S, D), dtype=np.float32)
    for core in range(8):
        b, h = core // 2, core % 2
        tiles = TILES[h]
        yb = np.asarray(res.results[core]["y"]).reshape(NPOS, 128, D)
        out[b].reshape(NKT, 128, D)[tiles] = yb
    return out
